# version 01
# baseline (speedup 1.0000x reference)
import contextlib
import numpy as np
import ml_dtypes
import concourse.bass as bass
import concourse.mybir as mybir
from concourse.bass_utils import run_bass_kernel_spmd

F32 = mybir.dt.float32
BF16 = mybir.dt.bfloat16
AF = mybir.ActivationFunctionType
ALU = mybir.AluOpType
AX = mybir.AxisListType

NCORES = 8
T = 4096
D = 1024
CH = 128
NCH = T // CH
PASS_TOK = 1024
NPASS = T // PASS_TOK
LNSCALE = float(np.log(1.0 / np.sqrt(128.0)))

ENGS = ("t", "v", "s", "p", "y")



class Op:
    __slots__ = ("eng", "fn", "deps", "dma", "idx", "reads", "writes")

    def __init__(self, eng, fn, deps, dma, idx, reads=(), writes=()):
        self.eng, self.fn, self.deps, self.dma, self.idx = eng, fn, deps, dma, idx
        self.reads, self.writes = tuple(reads), tuple(writes)


DEDUP = True


class Prog:

    def __init__(self, nc):
        self.nc = nc
        self.ops = []
        self.eng_ops = {e: [] for e in ENGS}
        self.writer = {}
        self.readers = {}
        self.floor = set()

    def emit(self, eng, fn, reads=(), writes=(), dma=False):
        idx = len(self.ops)
        deps = set(self.floor)
        for k in reads:
            w = self.writer.get(k)
            if w is not None:
                deps.add(w)
        for k in writes:
            w = self.writer.get(k)
            if w is not None:
                deps.add(w)
            for r in self.readers.get(k, ()):
                deps.add(r)
        deps.discard(idx)
        op = Op(eng, fn, deps, dma, idx, reads, writes)
        self.ops.append(op)
        self.eng_ops[eng].append(idx)
        for k in reads:
            self.readers.setdefault(k, []).append(idx)
        for k in writes:
            self.writer[k] = idx
            self.readers[k] = [idx]
        return idx

    def barrier(self):
        lasts = set()
        for e in ENGS:
            if self.eng_ops[e]:
                lasts.add(self.eng_ops[e][-1])
        self.floor = lasts

    def build(self, final_eng="p"):
        nc = self.nc
        ops = self.ops
        NSLOT = 8
        need = [False] * len(ops)
        dep_targets = set()
        for op in ops:
            for d in op.deps:
                dop = ops[d]
                if dop.dma or dop.eng != op.eng or op.eng != "t":
                    need[d] = True
                dep_targets.add(d)
        tail = []
        for i, op in enumerate(ops):
            if op.dma and i not in dep_targets:
                need[i] = True
                tail.append(i)
        stack = contextlib.ExitStack()
        sem_c = {e: stack.enter_context(nc.semaphore(f"sc{e}")) for e in ENGS}
        sem_d = {e: [stack.enter_context(nc.semaphore(f"sd{e}{i}"))
                     for i in range(NSLOT)] for e in ENGS}
        tok = [None] * len(ops)
        pre_wait = [None] * len(ops)
        cnt_c = {e: 0 for e in ENGS}
        dma_i = {e: 0 for e in ENGS}
        slot_uses = {e: [0] * NSLOT for e in ENGS}
        for op in ops:
            if op.dma:
                s = dma_i[op.eng] % NSLOT
                dma_i[op.eng] += 1
                u = slot_uses[op.eng][s]
                if u > 0:
                    pre_wait[op.idx] = (sem_d[op.eng][s], 16 * u)
                slot_uses[op.eng][s] = u + 1
                tok[op.idx] = (sem_d[op.eng][s], 16 * (u + 1))
            elif need[op.idx]:
                cnt_c[op.eng] += 1
                tok[op.idx] = (sem_c[op.eng], cnt_c[op.eng])
        waits = [None] * len(ops)
        hi = {}
        for op in ops:
            wl = []
            if pre_wait[op.idx] is not None:
                sem, v = pre_wait[op.idx]
                key = (op.eng, id(sem))
                if hi.get(key, 0) < v:
                    hi[key] = v
                    wl.append((sem, v))
            for d in sorted(op.deps):
                dop = ops[d]
                if (not dop.dma) and dop.eng == op.eng and op.eng == "t":
                    continue
                sem, v = tok[d]
                key = (op.eng, id(sem))
                if DEDUP and hi.get(key, 0) >= v:
                    continue
                hi[key] = max(hi.get(key, 0), v)
                wl.append((sem, v))
            waits[op.idx] = wl
        final_waits = []
        fhi = {}
        for i in sorted(tail):
            sem, v = tok[i]
            key = id(sem)
            if fhi.get(key, 0) >= v:
                continue
            fhi[key] = v
            final_waits.append((sem, v))

        with nc.Block() as block:
            def stream(e):
                def body(eng):
                    for i in self.eng_ops[e]:
                        for (sem, v) in waits[i]:
                            eng.wait_ge(sem, v)
                        inst = ops[i].fn(eng)
                        if ops[i].dma:
                            inst.then_inc(tok[i][0], 16)
                        elif tok[i] is not None:
                            inst.then_inc(tok[i][0], 1)
                    if e == final_eng:
                        for (sem, v) in final_waits:
                            eng.wait_ge(sem, v)
                return body
            block.tensor(stream("t"))
            block.vector(stream("v"))
            block.scalar(stream("s"))
            block.gpsimd(stream("p"))
            block.sync(stream("y"))
        self._semstack = stack
        return self


class Arena:
    def __init__(self, nc, margin=6 * 1024):
        self.nc = nc
        size = nc.sbuf_bytes_remaining - margin
        self.backing = nc.alloc_sbuf_tensor("arena0", [128, size], mybir.dt.uint8)
        base = nc.lookup_mloc(self.backing).addr
        self.off = base
        self.limit = base + size
        self.marks = []
        self.n = 0

    def alloc(self, name, shape, dtype, align=64):
        self.off = (self.off + align - 1) // align * align
        per_part = int(np.prod(shape[1:])) * mybir.dt.size(dtype)
        h = self.nc.alloc_sbuf_tensor_at(f"{name}_{self.n}", list(shape), dtype,
                                         offset=self.off)
        self.n += 1
        self.off += per_part
        if self.off > self.limit:
            raise RuntimeError(f"SBUF over budget: {self.off} at {name}")
        return h

    def mark(self):
        self.marks.append(self.off)

    def release(self):
        self.off = self.marks.pop()


def bcast_mid(h, part_stride, nrep, width, offset=0, part_count=128):
    return bass.AP(h, offset, [[part_stride, part_count], [0, nrep], [1, width]])


def rep_cols(h, part_stride, ncols, col_off, rep, part_count=128):
    return bass.AP(h, col_off, [[part_stride, part_count], [1, ncols], [0, rep]])


class Ker:

    def __init__(self, nc):
        self.nc = nc
        self.P = Prog(nc)
        self.ar = Arena(nc)
        self.pb = [nc.alloc_psum_tensor(f"pb{i}", [128, 512], F32) for i in range(6)]
        self.pbf = [nc.alloc_psum_tensor(f"pbf{i}", [128, 1024], BF16) for i in range(2)]
        self.io = {}
        self.C = {}

    def mm_group(self, bank, c0, c1, terms, reads):
        key = ("pb", bank)
        pb = self.pb[bank]
        n = len(terms)
        for i, (l, r) in enumerate(terms):
            self.P.emit("t", (lambda t, o=pb[:, c0:c1], L=l, R=r, st=(i == 0),
                              sp=(i == n - 1): t.matmul(o, L, R, start=st, stop=sp)),
                        reads=list(reads), writes=(key,))

    def mm_group_ap(self, out_ap, bank_key, terms, reads):
        n = len(terms)
        for i, (l, r) in enumerate(terms):
            self.P.emit("t", (lambda t, o=out_ap, L=l, R=r, st=(i == 0),
                              sp=(i == n - 1): t.matmul(o, L, R, start=st, stop=sp)),
                        reads=list(reads), writes=(bank_key,))

    def transpose(self, bfbank, col, src_ap, reads):
        key = ("pbf", bfbank)
        pb = self.pbf[bfbank]
        self.P.emit("t", (lambda t, o=pb[:, col:col + 128], i_=src_ap,
                          idm=self.C["id_bf"][:, :]: t.transpose(o, i_, idm)),
                    reads=list(reads) + ["id_bf"], writes=(key,))

    def ev(self, eng, fn, reads=(), writes=()):
        self.P.emit(eng, fn, reads=reads, writes=writes)

    def dma(self, q, out_ap, in_ap, reads=(), writes=()):
        self.P.emit(q, (lambda e, a=out_ap, b=in_ap: e.dma_start(a, b)),
                    reads=reads, writes=writes, dma=True)

    def dbg(self, name, src_ap, shape, dtype, reads):
        if not DEBUG:
            return
        h = self.nc.dram_tensor(f"dbg_{name}", list(shape), dtype,
                                kind="ExternalOutput").ap()
        self.dma("p", h, src_ap, reads=reads, writes=((f"dbg_{name}",),))

    def dma_t(self, q, out_ap, in_ap, reads=(), writes=()):
        self.P.emit(q, (lambda e, a=out_ap, b=in_ap: e.dma_start(a, b, transpose=True)),
                    reads=reads, writes=writes, dma=True)



def host_constants():
    c = {}
    jj = np.arange(1024, dtype=np.int64)
    ang = (np.outer(jj, jj) % 1024).astype(np.float64) * (2 * np.pi / 1024.0)
    c["c1"] = np.cos(ang).astype(ml_dtypes.bfloat16)
    c["c2"] = np.sin(ang).astype(ml_dtypes.bfloat16)
    c["c2n"] = (-np.sin(ang)).astype(ml_dtypes.bfloat16)
    s = np.arange(128)[:, None]
    t = np.arange(128)[None, :]
    c["ut01"] = (s <= t).astype(np.float32)
    c["cumm"] = (s <= t).astype(np.float32)
    c["maskd"] = np.where(s <= t, 0.0, -3.0e38).astype(np.float32)
    c["id_bf"] = np.eye(128, dtype=ml_dtypes.bfloat16)
    c["id_f32"] = np.eye(128, dtype=np.float32)
    half = 64
    inv_freq = 1.0 / (10000.0 ** (np.arange(half, dtype=np.float64) / half))
    freqs = np.arange(T, dtype=np.float64)[:, None] * inv_freq[None, :]
    c["ropecos"] = np.cos(freqs).astype(np.float32)
    c["ropesin"] = np.sin(freqs).astype(np.float32)
    return c


CONST_SPECS = {
    "c1": ([1024, 1024], BF16), "c2": ([1024, 1024], BF16), "c2n": ([1024, 1024], BF16),
    "ut01": ([128, 128], F32), "cumm": ([128, 128], F32), "maskd": ([128, 128], F32),
    "id_bf": ([128, 128], BF16), "id_f32": ([128, 128], F32),
    "ropecos": ([T, 64], F32), "ropesin": ([T, 64], F32),
}


def load_const(K, name):
    shape, dt = CONST_SPECS[name]
    h = K.ar.alloc(name, shape, dt)
    K.C[name] = h
    K.dma("y", h[:, :], K.io[name], writes=(name,))
    return h



def phase0_l0(K):
    nc, P, ar = K.nc, K.P, K.ar
    W = {}
    for nm in ("id_bf", "ut01"):
        load_const(K, nm)
    ones_bf = ar.alloc("ones_bf", [1, 512], BF16)
    K.C["ones_bf"] = ones_bf
    K.ev("p", lambda g: g.memset(ones_bf[:, :], 1.0), writes=("ones_bf",))

    for nm in ("wq0", "wk0", "wv0", "wo0"):
        cache = nc.dram_tensor(f"{nm}_c", [1024, 1024], BF16, kind="Internal").ap()
        K.dma("p", cache, K.io[nm], writes=(f"{nm}_c",))
        wt = ar.alloc(nm + "T", [128, 8, 1024], BF16)
        W[nm + "T"] = wt
        for dc in range(8):
            K.dma_t("y", wt[:, dc, :], cache[0:1024, 128 * dc:128 * dc + 128],
                    reads=(f"{nm}_c",), writes=((nm + "T", dc),))
    fbias = ar.alloc("fbias", [1, 1024], BF16)
    W["fbias"] = fbias
    K.dma("p", fbias[:, :], K.io["fbias"].rearrange("(a d) -> a d", a=1), writes=("fbias",))

    mc = ar.alloc("mc", [128, 8, 1024], BF16)
    W["mc"] = mc

    ar.mark()
    cc = {}
    for nm in ("c1", "c2", "c2n"):
        h = ar.alloc(nm, [128, 8, 1024], BF16)
        cc[nm] = h
        K.dma("y", h[:, :, :], K.io[nm].rearrange("(c p) o -> p c o", p=128),
              writes=(nm,))
    for nm, key in (("wfr", "wr"), ("wfi", "wi")):
        h = ar.alloc(key, [128, 8, 1024], BF16)
        cc[key] = h
        K.dma("p", h[:, :, :], K.io[nm].rearrange("(c p) o -> p c o", p=128),
              writes=(key,))
    prt = ar.alloc("prt", [128, 8, 1024], BF16)
    pit = ar.alloc("pit", [128, 8, 1024], BF16)

    def build_mat(dst, dst_key, lhs_pairs, scale):
        for ic in range(8):
            for jg in range(2):
                bank = jg
                terms = []
                for kc in range(8):
                    for (lh, lkey, rh, rkey) in lhs_pairs:
                        terms.append((lh[:, kc, 128 * ic:128 * ic + 128],
                                      rh[:, kc, 512 * jg:512 * jg + 512]))
                reads = [k for (_, k, _, _) in lhs_pairs] + [k for (_, _, _, k) in lhs_pairs]
                K.mm_group(bank, 0, 512, terms, reads)
                K.ev("s", (lambda s, o=dst[:, ic, 512 * jg:512 * jg + 512],
                           i_=K.pb[bank][:, 0:512], sc=scale:
                           s.activation(o, i_, AF.Copy, scale=np.float32(sc))),
                     reads=[("pb", bank)], writes=[(dst_key, ic, jg)])

    build_mat(prt, "prt", [(cc["wr"], "wr", cc["c1"], "c1"),
                           (cc["wi"], "wi", cc["c2"], "c2")], 1.0)
    build_mat(pit, "pit", [(cc["wi"], "wi", cc["c1"], "c1"),
                           (cc["wr"], "wr", cc["c2n"], "c2n")], 1.0)
    for jc in range(8):
        for og in range(2):
            bank = og
            terms = []
            for kc in range(8):
                terms.append((prt[:, kc, 128 * jc:128 * jc + 128],
                              cc["c1"][:, kc, 512 * og:512 * og + 512]))
                terms.append((pit[:, kc, 128 * jc:128 * jc + 128],
                              cc["c2n"][:, kc, 512 * og:512 * og + 512]))
            reads = [("prt", kc, g) for kc in range(8) for g in range(2)] + \
                    [("pit", kc, g) for kc in range(8) for g in range(2)] + ["c1", "c2n"]
            K.mm_group(bank, 0, 512, terms, reads)
            K.ev("s", (lambda s, o=mc[:, jc, 512 * og:512 * og + 512],
                       i_=K.pb[bank][:, 0:512]:
                       s.activation(o, i_, AF.Copy, scale=np.float32(1.0 / 1024.0))),
                 reads=[("pb", bank)], writes=[("mc", jc, og)])
    P.barrier()
    ar.release()
    return W



def ln_ops(K, src_ap, src_key, sq_scr, sq_key, cols, cols_key, eps):
    K.ev("s", (lambda s, o=sq_scr[:, :], i_=src_ap, a=cols[:, 1:2]:
               s.activation(o, i_, AF.Square, accum_out=a)),
         reads=[src_key], writes=[sq_key, (cols_key, "ssq")])
    K.ev("s", (lambda s, o=sq_scr[:, :], i_=src_ap, a=cols[:, 0:1]:
               s.activation(o, i_, AF.Identity, accum_out=a)),
         reads=[src_key], writes=[sq_key, (cols_key, "sx")])
    K.ev("v", (lambda v, o=cols[:, 2:3], i_=cols[:, 0:1]:
               v.tensor_scalar_mul(o, i_, 1.0 / D)),
         reads=[(cols_key, "sx")], writes=[(cols_key, "mean")])
    K.ev("v", (lambda v, o=cols[:, 4:5], i_=cols[:, 2:3], m=cols[:, 2:3]:
               v.tensor_scalar(o, i_, m, None, ALU.mult)),
         reads=[(cols_key, "mean")], writes=[(cols_key, "msq")])
    K.ev("v", (lambda v, o=cols[:, 5:6], i_=cols[:, 1:2], m=cols[:, 4:5]:
               v.tensor_scalar(o, i_, 1.0 / D, m, ALU.mult, ALU.subtract)),
         reads=[(cols_key, "ssq"), (cols_key, "msq")], writes=[(cols_key, "var")])
    K.ev("s", (lambda s, o=cols[:, 4:5], i_=cols[:, 5:6], b=eps:
               s.activation(o, i_, AF.Ln, bias=np.float32(b))),
         reads=[(cols_key, "var")], writes=[(cols_key, "lnv")])
    K.ev("s", (lambda s, o=cols[:, 3:4], i_=cols[:, 4:5]:
               s.activation(o, i_, AF.Exp, scale=np.float32(-0.5))),
         reads=[(cols_key, "lnv")], writes=[(cols_key, "rstd")])


P1LIM = 10 ** 9
P1ATT = 99
P1HEADS = 99
DEBUG = False
DBG_CI = 0


def phase1(K, W, moe_casts=None, nch=NCH):
    nc, P, ar = K.nc, K.P, K.ar
    sec = [0]

    def lim():
        sec[0] += 1
        return sec[0] <= P1LIM
    ar.mark()
    xin = [ar.alloc(f"xin{i}", [128, 1024], F32) for i in range(2)]
    lnsc = ar.alloc("lnsc", [128, 8], F32)
    sqs = ar.alloc("sqs", [128, 1024], F32)
    xnbf = ar.alloc("xnbf", [128, 1024], BF16)
    xnfm = ar.alloc("xnfm", [128, 8, 128], BF16)
    qpad = ar.alloc("qpad", [128, 16, 128], BF16)
    ktm = ar.alloc("ktm", [128, 1024], BF16)
    kpad = ar.alloc("kpad", [128, 16, 128], BF16)
    vtm = ar.alloc("vtm", [128, 16, 65], BF16)
    st0 = ar.alloc("st0", [128, 8, 65], F32)
    stbf = ar.alloc("stbf", [128, 8, 65], BF16)
    abuf = ar.alloc("abuf", [128, 8, 128], BF16)
    den = ar.alloc("den", [128, 16], F32)
    rden = ar.alloc("rden", [128, 16], F32)
    yall = ar.alloc("yall", [128, 1024], BF16)
    yfm = ar.alloc("yfm", [128, 8, 128], BF16)
    te1 = ar.alloc("te1", [128, 1024], F32)
    te2 = ar.alloc("te2", [128, 1024], F32)
    te3 = ar.alloc("te3", [128, 1024], F32)
    gbuf = ar.alloc("gbuf", [128, 1024], F32)
    x1t = ar.alloc("x1t", [128, 1024], F32)
    xn1f = ar.alloc("xn1f", [128, 1024], F32)
    xn1b = ar.alloc("xn1b", [128, 1024], BF16)

    K.ev("v", (lambda v, o=bass.AP(vtm, 64, [[16 * 65, 128], [65, 16], [1, 1]]):
               v.memset(o, 1.0)), writes=[("vtm", "ones")])
    K.ev("v", lambda v, o=st0[:, :, :]: v.memset(o, 0.0), writes=["st0"])
    K.ev("v", lambda v, o=qpad[:, :, :]: v.memset(o, 0.0), writes=["qpad"])
    K.ev("v", lambda v, o=kpad[:, :, :]: v.memset(o, 0.0), writes=["kpad"])


    for ci in range(nch):
        par = ci % 2
        xk = f"xin{par}"
        rows = slice(CH * ci, CH * ci + CH)
        K.dma("y", xin[par][:, :], K.io["x"][rows, :], writes=[xk])
        if lim():
            ln_ops(K, xin[par][:, :], xk, sqs, "sqs", lnsc, "lnsc", 1e-5)
            K.ev("v", (lambda v, o=xnbf[:, :], i_=xin[par][:, :], m=lnsc[:, 2:3],
                       r=lnsc[:, 3:4]:
                       v.tensor_scalar(o, i_, m, r, ALU.subtract, ALU.mult)),
                 reads=[xk, ("lnsc", "mean"), ("lnsc", "rstd")], writes=["xnbf"])
        if lim():
            for half in range(2):
                bb = half
                for q in range(4):
                    bc = 4 * half + q
                    K.transpose(bb, 128 * q, xnbf[:, 128 * bc:128 * bc + 128], ["xnbf"])
                K.ev("s", (lambda s, o=xnfm[:, 4 * half:4 * half + 4, :],
                           i_=K.pbf[bb][:, 0:512]: s.copy(o, i_)),
                     reads=[("pbf", bb)], writes=[("xnfm", half)])
            xnfm_keys = [("xnfm", 0), ("xnfm", 1)]
        if lim():
            for ec in range(8):
                bank = ec // 4
                c0 = 128 * (ec % 4)
                terms = [(W["wq0T"][:, kc, 128 * ec:128 * ec + 128], xnfm[:, kc, :])
                         for kc in range(8)]
                K.mm_group(bank, c0, c0 + 128, terms,
                           [("wq0T", kc) for kc in range(8)] + xnfm_keys)
        if lim():
            for bank in range(2):
                K.ev("v", (lambda v, o=te1[:, 512 * bank:512 * bank + 512],
                           i_=K.pb[bank][:, 0:512]:
                           v.tensor_scalar_min(o, i_, 0.0)),
                     reads=[("pb", bank)], writes=[("te1", bank)])
                K.ev("v", (lambda v, o=te2[:, 512 * bank:512 * bank + 512],
                           i_=K.pb[bank][:, 0:512]:
                           v.tensor_scalar_max(o, i_, 0.0)),
                     reads=[("pb", bank)], writes=[("te2", bank)])
            K.ev("s", (lambda s, o=te1[:, :], i_=te1[:, :]:
                       s.activation(o, i_, AF.Exp)),
                 reads=[("te1", 0), ("te1", 1)], writes=[("te1", 0), ("te1", 1)])
            K.ev("v", (lambda v, o=qpad[0:64, 0:16:2, :],
                       a=te1[0:64, :].rearrange("p (c w) -> p c w", w=128),
                       b=te2[0:64, :].rearrange("p (c w) -> p c w", w=128):
                       v.tensor_add(o, a, b)),
                 reads=[("te1", 0), ("te1", 1), ("te2", 0), ("te2", 1)],
                 writes=["qpad"])
            K.ev("v", (lambda v, o=qpad[64:128, 1:16:2, :],
                       a=te1[64:128, :].rearrange("p (c w) -> p c w", w=128),
                       b=te2[64:128, :].rearrange("p (c w) -> p c w", w=128):
                       v.tensor_add(o, a, b)),
                 reads=[("te1", 0), ("te1", 1), ("te2", 0), ("te2", 1)],
                 writes=["qpad"])
        if lim():
            for bank, eg in ((2, 0), (3, 1)):
                terms = [(xnfm[:, kc, :], W["wk0T"][:, kc, 512 * eg:512 * eg + 512])
                         for kc in range(8)]
                K.mm_group(bank, 0, 512, terms,
                           [("wk0T", kc) for kc in range(8)] + xnfm_keys)
            for bank in (2, 3):
                b = bank - 2
                K.ev("v", (lambda v, o=te1[:, 512 * b:512 * b + 512],
                           i_=K.pb[bank][:, 0:512]: v.tensor_scalar_min(o, i_, 0.0)),
                     reads=[("pb", bank)], writes=[("te1", b)])
                K.ev("v", (lambda v, o=te2[:, 512 * b:512 * b + 512],
                           i_=K.pb[bank][:, 0:512]: v.tensor_scalar_max(o, i_, 0.0)),
                     reads=[("pb", bank)], writes=[("te2", b)])
            K.ev("s", (lambda s, o=te1[:, :], i_=te1[:, :]: s.activation(o, i_, AF.Exp)),
                 reads=[("te1", 0), ("te1", 1)], writes=[("te1", 0), ("te1", 1)])
            K.ev("v", (lambda v, o=ktm[:, :], a=te1[:, :], b=te2[:, :]:
                       v.tensor_add(o, a, b)),
                 reads=[("te1", 0), ("te1", 1), ("te2", 0), ("te2", 1)], writes=["ktm"])
        if lim():
            for half in range(2):
                bb = half
                for q in range(4):
                    bc = 4 * half + q
                    K.transpose(bb, 128 * q, ktm[:, 128 * bc:128 * bc + 128], ["ktm"])
                K.ev("s", (lambda s, o=kpad[0:64, 8 * half:8 * half + 8:2, :],
                           i_=K.pbf[bb][0:64, 0:512].rearrange("p (c w) -> p c w", w=128):
                           s.copy(o, i_)),
                     reads=[("pbf", bb)], writes=["kpad"])
                K.ev("s", (lambda s, o=kpad[64:128, 8 * half + 1:8 * half + 8:2, :],
                           i_=K.pbf[bb][64:128, 0:512].rearrange("p (c w) -> p c w", w=128):
                           s.copy(o, i_)),
                     reads=[("pbf", bb)], writes=["kpad"])
            kfm_keys = ["kpad"]
        if lim():
            for bank, eg in ((4, 0), (5, 1)):
                terms = [(xnfm[:, kc, :], W["wv0T"][:, kc, 512 * eg:512 * eg + 512])
                         for kc in range(8)]
                K.mm_group(bank, 0, 512, terms,
                           [("wv0T", kc) for kc in range(8)] + xnfm_keys)
                K.ev("v", (lambda v, o=vtm[:, 8 * eg:8 * eg + 8, 0:64],
                           i_=K.pb[bank][:, 0:512].rearrange("p (h w) -> p h w", h=8):
                           v.tensor_copy(o, i_)),
                     reads=[("pb", bank)], writes=[("vtm", eg)])
            vtm_keys = [("vtm", 0), ("vtm", 1), ("vtm", "ones")]
        if lim():
            K.ev("s", (lambda s, o=stbf[:, :, :], i_=st0[:, :, :]: s.copy(o, i_)),
                 reads=["st0"], writes=["stbf"])
        if lim():
            for w in range(2):
                for hh in range(min(8, P1HEADS)):
                    h = 8 * w + hh
                    bank = hh // 4
                    c0 = 128 * (hh % 4)
                    lhsT = kpad[:, h, :]
                    rhs = qpad[:, h, :]
                    K.mm_group(bank, c0, c0 + 128, [(lhsT, rhs)], kfm_keys + ["qpad"])
                for bank in range(2) if P1ATT >= 2 else []:
                    K.ev("v", (lambda v, o=abuf[:, 4 * bank:4 * bank + 4, :],
                               i0=K.pb[bank][:, 0:512].rearrange("p (h w) -> p h w", h=4),
                               i1=bcast_mid(K.C["ut01"], 128, 4, 128):
                               v.tensor_tensor(o, i0, i1, ALU.mult)),
                         reads=[("pb", bank), "ut01"], writes=[("abuf", bank)])
                for hh in range(8) if P1ATT >= 3 else []:
                    h = 8 * w + hh
                    bank = 2 + hh // 4
                    c0 = 128 * (hh % 4)
                    terms = [(abuf[:, hh, :], vtm[:, h, :]),
                             (qpad[:, h, :], stbf[:, h // 2, :])]
                    K.mm_group(bank, c0, c0 + 65, terms,
                               [("abuf", hh // 4)] + vtm_keys + ["qpad", "stbf"])
                for bank in ((2, 3) if P1ATT >= 4 else ()):
                    b = bank - 2
                    K.ev("v", (lambda v, o=den[:, 8 * w + 4 * b:8 * w + 4 * b + 4],
                               i_=bass.AP(K.pb[bank], 64, [[512, 128], [128, 4]]):
                               v.tensor_copy(o, i_)),
                         reads=[("pb", bank)], writes=[("den", w, b)])
                if P1ATT >= 4:
                    K.ev("v", (lambda v, o=den[:, 8 * w:8 * w + 8], i_=den[:, 8 * w:8 * w + 8]:
                               v.tensor_scalar_max(o, i_, 1e-6)),
                         reads=[("den", w, 0), ("den", w, 1)], writes=[("den", w)])
                    K.ev("v", (lambda v, o=rden[:, 8 * w:8 * w + 8], i_=den[:, 8 * w:8 * w + 8]:
                               v.reciprocal(o, i_)),
                         reads=[("den", w)], writes=[("rden", w)])
                for bank in ((2, 3) if P1ATT >= 5 else ()):
                    b = bank - 2
                    K.ev("v", (lambda v,
                               o=bass.AP(yall, 512 * w + 256 * b, [[1024, 128], [64, 4], [1, 64]]),
                               i0=bass.AP(K.pb[bank], 0, [[512, 128], [128, 4], [1, 64]]),
                               i1=rep_cols(rden, 16, 4, 8 * w + 4 * b, 64):
                               v.tensor_tensor(o, i0, i1, ALU.mult)),
                         reads=[("pb", bank), ("rden", w)], writes=[("yall", w, b)])
                for hh in range(8) if P1ATT >= 6 else []:
                    h = 8 * w + hh
                    bank = 4 + hh // 4
                    j = hh % 4
                    c0 = 128 * j
                    terms = [(ktm[:, 128 * (h // 2):128 * (h // 2) + 128],
                              vtm[:, h, :])]
                    K.mm_group(bank, c0, c0 + 65, terms, ["ktm"] + vtm_keys)
                if P1ATT >= 7:
                    for hh in range(8):
                        h = 8 * w + hh
                        bank = 4 + hh // 4
                        j = hh % 4
                        hf = 64 * (h % 2)
                        K.ev("v", (lambda v,
                                   o=st0[hf:hf + 64, h // 2, :],
                                   i0=K.pb[bank][hf:hf + 64, 128 * j:128 * j + 65],
                                   i1=st0[hf:hf + 64, h // 2, :]:
                                   v.tensor_tensor(o, i0, i1, ALU.add)),
                             reads=[("pb", bank), "st0"], writes=["st0"])

            yall_keys = [("yall", w, b) for w in range(2) for b in range(2)]
        if lim():
            for half in range(2):
                bb = half
                for q in range(4):
                    bc = 4 * half + q
                    K.transpose(bb, 128 * q, yall[:, 128 * bc:128 * bc + 128], yall_keys)
                K.ev("s", (lambda s, o=yfm[:, 4 * half:4 * half + 4, :],
                           i_=K.pbf[bb][:, 0:512]: s.copy(o, i_)),
                     reads=[("pbf", bb)], writes=[("yfm", half)])
            yfm_keys = [("yfm", 0), ("yfm", 1)]
        if lim():
            for bank, eg in ((2, 0), (3, 1)):
                terms = [(yfm[:, kc, :], W["wo0T"][:, kc, 512 * eg:512 * eg + 512])
                         for kc in range(8)]
                K.mm_group(bank, 0, 512, terms,
                           [("wo0T", kc) for kc in range(8)] + yfm_keys)
        if lim():
            for bank, og in ((0, 0), (1, 1)):
                terms = [(xnfm[:, kc, :], W["mc"][:, kc, 512 * og:512 * og + 512])
                         for kc in range(8)]
                terms.append((K.C["ones_bf"][0:1, 0:128],
                              W["fbias"][0:1, 512 * og:512 * og + 512]))
                K.mm_group(bank, 0, 512, terms,
                           [("mc", kc, og) for kc in range(8)] + xnfm_keys +
                           ["fbias", "ones_bf"])
        if DEBUG and ci == DBG_CI:
            for bank in range(2):
                K.ev("s", (lambda s, o=te3[:, 512 * bank:512 * bank + 512],
                           i_=K.pb[bank][:, 0:512]: s.copy(o, i_)),
                     reads=[("pb", bank)], writes=["te3"])
            K.dbg("ffnz", te3[:, :], [128, 1024], F32, ["te3"])
            for bank in (2, 3):
                K.ev("s", (lambda s, o=te2[:, 512 * (bank - 2):512 * (bank - 2) + 512],
                           i_=K.pb[bank][:, 0:512]: s.copy(o, i_)),
                     reads=[("pb", bank)], writes=[("te2", bank - 2)])
            K.dbg("stm", te2[:, :], [128, 1024], F32, [("te2", 0), ("te2", 1)])
            K.dbg("yfm", yfm[:, :, :], [128, 8, 128], BF16, [("yfm", 0), ("yfm", 1)])
        if lim():
            for bank in range(2):
                K.ev("v", (lambda v, o=te1[:, 512 * bank:512 * bank + 512],
                           i_=K.pb[bank][:, 0:512]: v.tensor_scalar_mul(o, i_, -1.0)),
                     reads=[("pb", bank)], writes=[("te1", bank)])
                K.ev("v", (lambda v, o=te2[:, 512 * bank:512 * bank + 512],
                           i0=K.pb[bank][:, 0:512], i1=te1[:, 512 * bank:512 * bank + 512]:
                           v.tensor_tensor(o, i0, i1, ALU.min)),
                     reads=[("pb", bank), ("te1", bank)], writes=[("te2", bank)])
            K.ev("s", (lambda s, o=te2[:, :], i_=te2[:, :]: s.activation(o, i_, AF.Exp)),
                 reads=[("te2", 0), ("te2", 1)], writes=[("te2", 0), ("te2", 1)])
            K.ev("s", (lambda s, o=te3[:, :], i_=te2[:, :]:
                       s.activation(o, i_, AF.Ln, bias=np.float32(1.0))),
                 reads=[("te2", 0), ("te2", 1)], writes=["te3"])
            K.ev("v", (lambda v, o=te1[:, :], i_=te1[:, :]: v.tensor_scalar_max(o, i_, 0.0)),
                 reads=[("te1", 0), ("te1", 1)], writes=[("te1", 0), ("te1", 1)])
            K.ev("v", (lambda v, o=te3[:, :], a=te1[:, :], b=te3[:, :]:
                       v.tensor_add(o, a, b)),
                 reads=[("te1", 0), ("te1", 1), "te3"], writes=["te3"])
            K.ev("s", (lambda s, o=te3[:, :], i_=te3[:, :]:
                       s.activation(o, i_, AF.Exp, scale=np.float32(-1.0))),
                 reads=["te3"], writes=["te3"])
            for bank in range(2):
                K.ev("v", (lambda v, o=gbuf[:, 512 * bank:512 * bank + 512],
                           i0=K.pb[bank][:, 0:512], i1=te3[:, 512 * bank:512 * bank + 512]:
                           v.tensor_tensor(o, i0, i1, ALU.mult)),
                     reads=[("pb", bank), "te3"], writes=[("gbuf", bank)])
        if lim():
            for bank in (2, 3):
                b = bank - 2
                K.ev("v", (lambda v, o=te1[:, 512 * b:512 * b + 512],
                           i0=K.pb[bank][:, 0:512], i1=gbuf[:, 512 * b:512 * b + 512]:
                           v.tensor_tensor(o, i0, i1, ALU.mult)),
                     reads=[("pb", bank), ("gbuf", b)], writes=[("te1", b)])
            K.ev("v", (lambda v, o=x1t[:, :], a=xin[par][:, :], b=te1[:, :]:
                       v.tensor_add(o, a, b)),
                 reads=[xk, ("te1", 0), ("te1", 1)], writes=["x1t"])
        if lim():
            ln_ops(K, x1t[:, :], "x1t", sqs, "sqs", lnsc, "lnsc", 1e-5)
            K.ev("v", (lambda v, o=xn1f[:, :], i_=x1t[:, :], m=lnsc[:, 2:3],
                       r=lnsc[:, 3:4]:
                       v.tensor_scalar(o, i_, m, r, ALU.subtract, ALU.mult)),
                 reads=["x1t", ("lnsc", "mean"), ("lnsc", "rstd")], writes=["xn1f"])
            K.ev("v", (lambda v, o=xn1b[:, :], i_=xn1f[:, :]: v.tensor_copy(o, i_)),
                 reads=["xn1f"], writes=["xn1b"])
        if ci == DBG_CI:
            K.dbg("xnbf", xnbf[:, :], [128, 1024], BF16, ["xnbf"])
            K.dbg("qpad", qpad[:, :, :], [128, 16, 128], BF16, ["qpad"])
            K.dbg("kpad", kpad[:, :, :], [128, 16, 128], BF16, ["kpad"])
            K.dbg("ktm", ktm[:, :], [128, 1024], BF16, ["ktm"])
            K.dbg("vtm", vtm[:, :, :], [128, 16, 65], BF16,
                  [("vtm", 0), ("vtm", 1), ("vtm", "ones")])
            K.dbg("den", den[:, :], [128, 16], F32, [("den", 0), ("den", 1)])
            K.dbg("yall", yall[:, :], [128, 1024], BF16,
                  [("yall", w2, b2) for w2 in range(2) for b2 in range(2)])
            K.dbg("gbuf", gbuf[:, :], [128, 1024], F32, [("gbuf", 0), ("gbuf", 1)])
            K.dbg("st0", st0[:, :, :], [128, 8, 65], F32, ["st0"])
        K.dma("p", K.io["x1"][rows, :], x1t[:, :], reads=["x1t"], writes=[("x1_h", ci)])
        K.dma("p", K.io["xn1f"][rows, :], xn1f[:, :], reads=["xn1f"],
              writes=[("xn1f_h", ci)])
        K.dma("p", K.io["xn1b"][rows, :], xn1b[:, :], reads=["xn1b"],
              writes=[("xn1b_h", ci)])
        if moe_casts:
            while moe_casts and len(moe_casts) > (NCH - 1 - ci) * 2:
                moe_casts.pop(0)()

    P.barrier()
    ar.release()
    return moe_casts


def moe_cast_list(K):
    nc = K.nc
    thunks = []
    K.moe_cache = {}
    w1c = nc.dram_tensor("w1c", [9, 2048, 1024], BF16, kind="Internal").ap()
    w3c = nc.dram_tensor("w3c", [9, 2048, 1024], BF16, kind="Internal").ap()
    w2c = nc.dram_tensor("w2c", [9, 1024, 1024], BF16, kind="Internal").ap()
    K.moe_cache.update(w1c=w1c, w3c=w3c, w2c=w2c)

    def mk(dst_ap, src_ap, key):
        def th():
            K.dma("p", dst_ap, src_ap, writes=[key])
        return th

    for e in range(8):
        thunks.append(mk(w1c[e, 0:1024, :], K.io["w1"][e, 0:2048:2, :], ("w1c", e, 0)))
        thunks.append(mk(w1c[e, 1024:2048, :], K.io["w1"][e, 1:2048:2, :], ("w1c", e, 1)))
        thunks.append(mk(w3c[e, 0:1024, :], K.io["w3"][e, 0:2048:2, :], ("w3c", e, 0)))
        thunks.append(mk(w3c[e, 1024:2048, :], K.io["w3"][e, 1:2048:2, :], ("w3c", e, 1)))
        thunks.append(mk(w2c[e, :, :], K.io["w2"][e, :, :], ("w2c", e)))
    thunks.append(mk(w1c[8, 0:1024, :], K.io["sw1"][0:2048:2, :], ("w1c", 8, 0)))
    thunks.append(mk(w1c[8, 1024:2048, :], K.io["sw1"][1:2048:2, :], ("w1c", 8, 1)))
    thunks.append(mk(w3c[8, 0:1024, :], K.io["sw3"][0:2048:2, :], ("w3c", 8, 0)))
    thunks.append(mk(w3c[8, 1024:2048, :], K.io["sw3"][1:2048:2, :], ("w3c", 8, 1)))
    thunks.append(mk(w2c[8, :, :], K.io["sw2"][:, :], ("w2c", 8)))
    return thunks





def phase2_prep(K):
    nc, P, ar = K.nc, K.P, K.ar
    W = {}
    for nm in ("cumm", "maskd", "id_f32", "id_bf"):
        if nm not in K.C:
            load_const(K, nm)
    for nm in ("ropecos", "ropesin"):
        h = ar.alloc(nm, [128, 32, 64], F32)
        K.C[nm] = h
        K.dma("y", h[:, :, :], K.io[nm].rearrange("(c p) i -> p c i", p=128),
              writes=(nm,))
    ones_f32 = ar.alloc("ones_f32", [128, 128], F32)
    K.C["ones_f32"] = ones_f32
    K.ev("p", lambda g: g.memset(ones_f32[:, :], 1.0), writes=("ones_f32",))
    if "ones_bf" not in K.C:
        ones_bf = ar.alloc("ones_bf", [1, 512], BF16)
        K.C["ones_bf"] = ones_bf
        K.ev("p", lambda g: g.memset(ones_bf[:, :], 1.0), writes=("ones_bf",))

    nw = ar.alloc("nw", [128, 8], F32)
    for dc in range(8):
        K.dma("p", nw[:, dc:dc + 1],
              K.io["norm_w"][128 * dc:128 * dc + 128].rearrange("(p a) -> p a", a=1),
              writes=(("nw", dc),))
    nw_keys = [("nw", dc) for dc in range(8)]

    for nm, scale_nw in (("wq1", True), ("wk1", True), ("wv1", True),
                         ("wout1", False)):
        cache = nc.dram_tensor(f"{nm}_c", [1024, 1024], BF16, kind="Internal").ap()
        K.dma("p", cache, K.io[nm], writes=(f"{nm}_c",))
        wt = ar.alloc(nm + "T", [128, 8, 1024], BF16)
        W[nm + "T"] = wt
        for dc in range(8):
            K.dma_t("y", wt[:, dc, :], cache[0:1024, 128 * dc:128 * dc + 128],
                    reads=(f"{nm}_c",), writes=((nm + "T", dc),))
            if scale_nw:
                K.ev("v", (lambda v, o=wt[:, dc, :], i_=wt[:, dc, :],
                           s1=nw[:, dc:dc + 1]:
                           v.tensor_scalar_mul(o, i_, s1)),
                     reads=[(nm + "T", dc), ("nw", dc)], writes=[(nm + "T", dc)])

    gcache = nc.dram_tensor("g_c", [32, 1024], BF16, kind="Internal").ap()
    for j, nm in enumerate(("wbeta", "wig", "wog", "gate_w")):
        K.dma("p", gcache[8 * j:8 * j + 8, :], K.io[nm], writes=(("g_c", j),))
    gatesT = ar.alloc("gatesT", [128, 8, 32], BF16)
    W["gatesT"] = gatesT
    for dc in range(8):
        K.dma_t("y", gatesT[:, dc, :], gcache[0:32, 128 * dc:128 * dc + 128],
                reads=[("g_c", j) for j in range(4)], writes=(("gatesT", dc),))
        K.ev("v", (lambda v, o=gatesT[:, dc, 0:24], i_=gatesT[:, dc, 0:24],
                   s1=nw[:, dc:dc + 1]: v.tensor_scalar_mul(o, i_, s1)),
             reads=[("gatesT", dc), ("nw", dc)], writes=[("gatesT", dc)])

    gbias = ar.alloc("gbias", [1, 24], BF16)
    W["gbias"] = gbias
    for j, nm in enumerate(("bbeta", "big", "bog")):
        K.dma("p", gbias[0:1, 8 * j:8 * j + 8],
              K.io[nm].rearrange("(a e) -> a e", a=1), writes=(("gbias", j),))
    bout = ar.alloc("bout", [1, 1024], BF16)
    W["bout"] = bout
    K.dma("p", bout[:, :], K.io["bout"].rearrange("(a d) -> a d", a=1),
          writes=("bout",))
    gbrow = ar.alloc("gbrow", [1, 8], F32)
    K.dma("y", gbrow[:, :], K.io["gate_b"].rearrange("(a e) -> a e", a=1),
          writes=("gbrow",))
    gb_bc = ar.alloc("gb_bc", [128, 8], F32)
    W["gb_bc"] = gb_bc
    K.mm_group_ap(K.pb[2][:, 0:8], ("pb", 2),
                  [(ones_f32[0:1, :], gbrow[0:1, :])], ["ones_f32", "gbrow"])
    K.ev("v", (lambda v, o=gb_bc[:, :], i_=K.pb[2][:, 0:8]: v.tensor_copy(o, i_)),
         reads=[("pb", 2)], writes=["gb_bc"])

    M_sb = ar.alloc("M_sb", [128, 8, 128], F32)
    M_bf = ar.alloc("M_bf", [128, 8, 128], BF16)
    W["M_sb"], W["M_bf"] = M_sb, M_bf
    K.ev("v", lambda v, o=M_sb[:, :, :]: v.memset(o, 0.0), writes=["M_sb"])
    cumsc = ar.alloc("cumsc", [128, 128], F32)
    W["cumsc"] = cumsc
    K.ev("v", lambda v, o=cumsc[:, :]: v.memset(o, 0.0), writes=["cumsc"])
    return W


def phase2(K, W, moe_casts=None, nch=NCH):
    nc, P, ar = K.nc, K.P, K.ar
    ar.mark()
    xin = [ar.alloc(f"p2xin{i}", [128, 1024], F32) for i in range(2)]
    sc = ar.alloc("p2sc", [128, 8], F32)
    sqs = ar.alloc("p2sqs", [128, 1024], F32)
    r_bf = ar.alloc("r_bf", [128, 1024], BF16)
    rfm = ar.alloc("rfm", [128, 8, 128], BF16)
    q_ro = ar.alloc("q_ro", [128, 1024], BF16)
    k_ro = ar.alloc("k_ro", [128, 1024], BF16)
    v_bf = ar.alloc("v_bf", [128, 1024], BF16)
    khat = ar.alloc("khat", [128, 1024], BF16)
    qfm2 = ar.alloc("qfm2", [128, 8, 128], BF16)
    kfm2 = ar.alloc("kfm2", [128, 8, 128], BF16)
    zg = ar.alloc("zg", [128, 24], F32)
    nl = ar.alloc("nl", [128, 16], F32)
    og = ar.alloc("og", [128, 8], F32)
    cum = ar.alloc("cum", [128, 8], F32)
    colst = ar.alloc("colst", [128, 8], F32)
    ncC = ar.alloc("ncC", [128, 8], F32)
    wcol = ar.alloc("wcol", [128, 8], F32)
    decC = ar.alloc("decC", [128, 8], F32)
    expc = ar.alloc("expc", [128, 8], F32)
    cumT = ar.alloc("cumT", [128, 3, 128], F32)
    dsb = ar.alloc("dsb", [128, 8, 128], F32)
    abuf2 = ar.alloc("abuf2", [128, 8, 128], BF16)
    itmp = ar.alloc("itmp", [128, 1024], F32)
    y0 = ar.alloc("y0", [128, 1024], F32)
    y_bf = ar.alloc("y_bf", [128, 1024], BF16)
    yfm2 = ar.alloc("yfm2", [128, 8, 128], BF16)
    s1t = ar.alloc("s1t", [128, 1024], F32)
    te1 = ar.alloc("p2te1", [128, 1024], F32)
    te2 = ar.alloc("p2te2", [128, 1024], F32)
    lg = ar.alloc("lg", [128, 48], F32)
    cwt = ar.alloc("cwt", [128, 8], F32)
    idxc = ar.alloc("idxc", [128, 8], F32)
    idxp = ar.alloc("idxp", [128, 8], F32)
    K.ev("p", (lambda g, o=idxc[:, :]: g.iota(o, [[1, 8]], channel_multiplier=0,
                                              allow_small_or_imprecise_dtypes=True)),
         writes=["idxc"])
    K.ev("v", (lambda v, o=idxp[:, :], i_=idxc[:, :]:
               v.tensor_scalar_add(o, i_, 1.0e4)),
         reads=["idxc"], writes=["idxp"])
    CUMM = K.C["cumm"]
    MASKD = K.C["maskd"]
    ONESF = K.C["ones_f32"]

    base_cls_bank = {0: 2, 1: 4, 2: 5}

    for ci in range(nch):
        par = ci % 2
        xk = f"p2xin{par}"
        rows = slice(CH * ci, CH * ci + CH)
        K.dma("y", xin[par][:, :], K.io["xn1f"][rows, :], writes=[xk])
        K.ev("s", (lambda s, o=sqs[:, :], i_=xin[par][:, :], a=sc[:, 0:1]:
                   s.activation(o, i_, AF.Square, accum_out=a)),
             reads=[xk], writes=["p2sqs", ("sc", 0)])
        K.ev("s", (lambda s, o=sc[:, 1:2], i_=sc[:, 0:1]:
                   s.activation(o, i_, AF.Ln, scale=np.float32(1.0 / D),
                                bias=np.float32(1e-6))),
             reads=[("sc", 0)], writes=[("sc", 1)])
        K.ev("s", (lambda s, o=sc[:, 2:3], i_=sc[:, 1:2]:
                   s.activation(o, i_, AF.Exp, scale=np.float32(-0.5))),
             reads=[("sc", 1)], writes=[("sc", 2)])
        K.ev("v", (lambda v, o=sc[:, 3:4], i_=sc[:, 2:3]: v.reciprocal(o, i_)),
             reads=[("sc", 2)], writes=[("sc", 3)])
        K.ev("v", (lambda v, o=r_bf[:, :], i_=xin[par][:, :], s1=sc[:, 2:3]:
                   v.tensor_scalar_mul(o, i_, s1)),
             reads=[xk, ("sc", 2)], writes=["r_bf"])
        for half in range(2):
            bb = half
            for q in range(4):
                bc = 4 * half + q
                K.transpose(bb, 128 * q, r_bf[:, 128 * bc:128 * bc + 128], ["r_bf"])
            K.ev("s", (lambda s, o=rfm[:, 4 * half:4 * half + 4, :],
                       i_=K.pbf[bb][:, 0:512]: s.copy(o, i_)),
                 reads=[("pbf", bb)], writes=[("rfm", half)])
        rfm_keys = [("rfm", 0), ("rfm", 1)]

        g_reads = [("gatesT", kc) for kc in range(8)] + rfm_keys + \
                  ["ones_bf", ("gbias", 0), ("gbias", 1), ("gbias", 2)]
        for kc in range(8):
            K.P.emit("t", (lambda t, o=K.pb[3][:, 0:32], L=rfm[:, kc, :],
                           R=W["gatesT"][:, kc, :], st=(kc == 0):
                           t.matmul(o, L, R, start=st, stop=False)),
                     reads=g_reads, writes=[("pb", 3)])
        K.P.emit("t", (lambda t, o=K.pb[3][:, 0:24],
                       L=K.C["ones_bf"][0:1, 0:128], R=W["gbias"][0:1, :]:
                       t.matmul(o, L, R, start=False, stop=True)),
                 reads=g_reads, writes=[("pb", 3)])
        K.ev("v", (lambda v, o=zg[:, :], i_=K.pb[3][:, 0:24]: v.tensor_copy(o, i_)),
             reads=[("pb", 3)], writes=["zg"])
        K.ev("s", (lambda s, o=lg[:, 0:8], i_=K.pb[3][:, 24:32], sca=sc[:, 3:4]:
                   s.activation(o, i_, AF.Copy, scale=sca)),
             reads=[("pb", 3), ("sc", 3)], writes=[("lg", "logits")])

        K.ev("v", (lambda v, o=te1[:, 0:16], i_=zg[:, 0:16]:
                   v.tensor_scalar_mul(o, i_, -1.0)),
             reads=["zg"], writes=[("te1", "zn")])
        K.ev("v", (lambda v, o=te1[:, 16:32], i0=zg[:, 0:16], i1=te1[:, 0:16]:
                   v.tensor_tensor(o, i0, i1, ALU.min)),
             reads=["zg", ("te1", "zn")], writes=[("te1", "na")])
        K.ev("s", (lambda s, o=te1[:, 16:32], i_=te1[:, 16:32]:
                   s.activation(o, i_, AF.Exp)),
             reads=[("te1", "na")], writes=[("te1", "na")])
        K.ev("s", (lambda s, o=te1[:, 32:48], i_=te1[:, 16:32]:
                   s.activation(o, i_, AF.Ln, bias=np.float32(1.0))),
             reads=[("te1", "na")], writes=[("te1", "l")])
        K.ev("v", (lambda v, o=te1[:, 0:16], i_=te1[:, 0:16]:
                   v.tensor_scalar_max(o, i_, 0.0)),
             reads=[("te1", "zn")], writes=[("te1", "zn")])
        K.ev("v", (lambda v, o=nl[:, :], a=te1[:, 0:16], b=te1[:, 32:48]:
                   v.tensor_add(o, a, b)),
             reads=[("te1", "zn"), ("te1", "l")], writes=["nl"])
        K.ev("v", (lambda v, o=nl[:, :], i_=nl[:, :]:
                   v.tensor_scalar_min(o, i_, 80.0)),
             reads=["nl"], writes=["nl"])
        K.ev("s", (lambda s, o=og[:, :], i_=zg[:, 16:24]:
                   s.activation(o, i_, AF.Exp, scale=np.float32(-1.0))),
             reads=["zg"], writes=["og"])
        K.ev("v", (lambda v, o=og[:, :], i_=og[:, :]: v.tensor_scalar_add(o, i_, 1.0)),
             reads=["og"], writes=["og"])
        K.ev("v", (lambda v, o=og[:, :], i_=og[:, :]: v.reciprocal(o, i_)),
             reads=["og"], writes=["og"])

        K.mm_group_ap(K.pb[3][:, 64:72], ("pb", 3),
                      [(CUMM[:, :], nl[:, 0:8])], ["cumm", "nl"])
        K.ev("v", (lambda v, o=cum[:, :], i_=K.pb[3][:, 64:72]: v.tensor_copy(o, i_)),
             reads=[("pb", 3)], writes=["cum"])
        K.ev("v", (lambda v, o=colst[:, :], a=cum[:, :], b=nl[:, 8:16]:
                   v.tensor_sub(o, a, b)),
             reads=["cum", "nl"], writes=["colst"])
        K.ev("v", (lambda v, o=colst[:, :], i_=colst[:, :]:
                   v.tensor_scalar_add(o, i_, LNSCALE)),
             reads=["colst"], writes=["colst"])
        K.ev("s", (lambda s, o=expc[:, :], i_=cum[:, :]:
                   s.activation(o, i_, AF.Exp, scale=np.float32(-1.0))),
             reads=["cum"], writes=["expc"])

        for g in range(3):
            n_in = 3 if g < 2 else 2
            K.ev("v", (lambda v,
                       o=bass.AP(W["cumsc"], 0, [[128, 128], [32, n_in]]),
                       i_=cum[:, 3 * g:3 * g + n_in]:
                       v.tensor_scalar_mul(o, i_, -1.0)),
                 reads=["cum"], writes=["cumsc"])
            K.P.emit("t", (lambda t, o=K.pb[3][:, 128 + 128 * g:256 + 128 * g],
                           i_=W["cumsc"][:, :], idm=K.C["id_f32"][:, :]:
                           t.transpose(o, i_, idm)),
                     reads=["cumsc", "id_f32"], writes=[("pb", 3)])
            K.ev("s", (lambda s, o=cumT[:, g, :],
                       i_=K.pb[3][:, 128 + 128 * g:256 + 128 * g]: s.copy(o, i_)),
                 reads=[("pb", 3)], writes=[("cumT", g)])
        cumT_keys = [("cumT", g) for g in range(3)]

        for h in range(8):
            g, m = h // 3, h % 3
            bank = base_cls_bank[m]
            K.mm_group_ap(K.pb[bank][:, 384 + h % 4:385 + h % 4], ("pb", bank),
                          [(ONESF[32 * m:32 * m + 1, :],
                            cumT[32 * m:32 * m + 1, g, 127:128])],
                          ["ones_f32"] + cumT_keys)
            K.ev("v", (lambda v, o=ncC[:, h:h + 1],
                       i_=K.pb[bank][:, 384 + h % 4:385 + h % 4]:
                       v.tensor_copy(o, i_)),
                 reads=[("pb", bank)], writes=[("ncC", h)])
        ncC_keys = [("ncC", h) for h in range(8)]
        K.ev("v", (lambda v, o=wcol[:, :], a=colst[:, :], b=ncC[:, :]:
                   v.tensor_add(o, a, b)),
             reads=["colst"] + ncC_keys, writes=["wcol"])
        K.ev("s", (lambda s, o=wcol[:, :], i_=wcol[:, :]:
                   s.activation(o, i_, AF.Exp)),
             reads=["wcol"], writes=["wcol"])
        K.ev("s", (lambda s, o=decC[:, :], i_=ncC[:, :]:
                   s.activation(o, i_, AF.Exp)),
             reads=ncC_keys, writes=["decC"])

        for h in range(8):
            g, m = h // 3, h % 3
            bank = base_cls_bank[m]
            c0 = 128 * (h % 3 if m != 0 else (h // 3))
            slot = {0: h // 3, 1: h // 3, 2: h // 3}[m]
            c0 = 128 * slot
            K.mm_group_ap(K.pb[bank][:, c0:c0 + 128], ("pb", bank),
                          [(ONESF[32 * m:32 * m + 1, :],
                            cumT[32 * m:32 * m + 1, g, :])],
                          ["ones_f32"] + cumT_keys)
            K.ev("v", (lambda v, o=dsb[:, h, :], i0=K.pb[bank][:, c0:c0 + 128],
                       i1=MASKD[:, :]: v.tensor_tensor(o, i0, i1, ALU.add)),
                 reads=[("pb", bank), "maskd"], writes=[("dsb", h)])
            K.ev("s", (lambda s, o=dsb[:, h, :], i_=dsb[:, h, :],
                       b=colst[:, h:h + 1]:
                       s.activation(o, i_, AF.Exp, bias=b)),
                 reads=[("dsb", h), "colst"], writes=[("dsb", h)])
        dsb_keys = [("dsb", h) for h in range(8)]

        def proj_tm(wt_name, out_tile, out_key, rope):
            for bank, eg in ((0, 0), (1, 1)):
                terms = [(rfm[:, kc, :], W[wt_name][:, kc, 512 * eg:512 * eg + 512])
                         for kc in range(8)]
                K.mm_group(bank, 0, 512, terms,
                           [(wt_name, kc) for kc in range(8)] + rfm_keys)
            if not rope:
                for bank in range(2):
                    K.ev("v", (lambda v, o=out_tile[:, 512 * bank:512 * bank + 512],
                               i_=K.pb[bank][:, 0:512]: v.tensor_copy(o, i_)),
                         reads=[("pb", bank)], writes=[out_key])
                return
            cos = K.C["ropecos"]
            sin = K.C["ropesin"]
            for bank in range(2):
                ps = K.pb[bank]
                qq1 = bass.AP(ps, 0, [[512, 128], [128, 4], [1, 64]])
                qq2 = bass.AP(ps, 64, [[512, 128], [128, 4], [1, 64]])
                cosb = bass.AP(cos, 64 * ci, [[2048, 128], [0, 4], [1, 64]])
                sinb = bass.AP(sin, 64 * ci, [[2048, 128], [0, 4], [1, 64]])
                o1 = bass.AP(out_tile, 512 * bank, [[1024, 128], [128, 4], [1, 64]])
                o2 = bass.AP(out_tile, 512 * bank + 64, [[1024, 128], [128, 4], [1, 64]])
                t1 = te1[:, 0:256].rearrange("p (c w) -> p c w", w=64)
                t2 = te1[:, 256:512].rearrange("p (c w) -> p c w", w=64)
                t3 = te2[:, 0:256].rearrange("p (c w) -> p c w", w=64)
                t4 = te2[:, 256:512].rearrange("p (c w) -> p c w", w=64)
                K.ev("v", (lambda v, o=t1, i0=qq1, i1=cosb:
                           v.tensor_tensor(o, i0, i1, ALU.mult)),
                     reads=[("pb", bank), "ropecos"], writes=[("p2te1", 0)])
                K.ev("v", (lambda v, o=t2, i0=qq2, i1=sinb:
                           v.tensor_tensor(o, i0, i1, ALU.mult)),
                     reads=[("pb", bank), "ropesin"], writes=[("p2te1", 1)])
                K.ev("v", (lambda v, o=o1, a=t1, b=t2: v.tensor_sub(o, a, b)),
                     reads=[("p2te1", 0), ("p2te1", 1)], writes=[out_key])
                K.ev("v", (lambda v, o=t3, i0=qq1, i1=sinb:
                           v.tensor_tensor(o, i0, i1, ALU.mult)),
                     reads=[("pb", bank), "ropesin"], writes=[("p2te2", 0)])
                K.ev("v", (lambda v, o=t4, i0=qq2, i1=cosb:
                           v.tensor_tensor(o, i0, i1, ALU.mult)),
                     reads=[("pb", bank), "ropecos"], writes=[("p2te2", 1)])
                K.ev("v", (lambda v, o=o2, a=t3, b=t4: v.tensor_add(o, a, b)),
                     reads=[("p2te2", 0), ("p2te2", 1)], writes=[out_key])

        proj_tm("wq1T", q_ro, "q_ro", True)
        for half in range(2):
            bb = half
            for q in range(4):
                bc = 4 * half + q
                K.transpose(bb, 128 * q, q_ro[:, 128 * bc:128 * bc + 128], ["q_ro"])
            K.ev("s", (lambda s, o=qfm2[:, 4 * half:4 * half + 4, :],
                       i_=K.pbf[bb][:, 0:512]: s.copy(o, i_)),
                 reads=[("pbf", bb)], writes=["qfm2"])
        proj_tm("wk1T", k_ro, "k_ro", True)
        for half in range(2):
            bb = half
            for q in range(4):
                bc = 4 * half + q
                K.transpose(bb, 128 * q, k_ro[:, 128 * bc:128 * bc + 128], ["k_ro"])
            K.ev("s", (lambda s, o=kfm2[:, 4 * half:4 * half + 4, :],
                       i_=K.pbf[bb][:, 0:512]: s.copy(o, i_)),
                 reads=[("pbf", bb)], writes=["kfm2"])
        proj_tm("wv1T", v_bf, "v_bf", False)

        K.ev("s", (lambda s, o=W["M_bf"][:, :, :], i_=W["M_sb"][:, :, :]:
                   s.copy(o, i_)),
             reads=["M_sb"], writes=["M_bf"])

        for h in range(8):
            bank = h // 4
            c0 = 128 * (h % 4)
            K.mm_group(bank, c0, c0 + 128,
                       [(kfm2[:, h, :], qfm2[:, h, :])], ["kfm2", "qfm2"])
        for bank in range(2):
            K.ev("v", (lambda v, o=abuf2[:, 4 * bank:4 * bank + 4, :],
                       i0=K.pb[bank][:, 0:512].rearrange("p (h w) -> p h w", h=4),
                       i1=dsb[:, 4 * bank:4 * bank + 4, :]:
                       v.tensor_tensor(o, i0, i1, ALU.mult)),
                 reads=[("pb", bank)] + dsb_keys, writes=[("abuf2", bank)])

        for h in range(8):
            bank = 2 + h // 4
            c0 = 128 * (h % 4)
            K.mm_group(bank, c0, c0 + 128,
                       [(abuf2[:, h, :], v_bf[:, 128 * h:128 * h + 128])],
                       [("abuf2", h // 4), "v_bf"])
            bank2 = 4 + h // 4
            K.mm_group(bank2, c0, c0 + 128,
                       [(qfm2[:, h, :], W["M_bf"][:, h, :])], ["qfm2", "M_bf"])
        for h in range(8):
            bank2 = 4 + h // 4
            c0 = 128 * (h % 4)
            K.ev("s", (lambda s, o=itmp[:, 128 * h:128 * h + 128],
                       i_=K.pb[bank2][:, c0:c0 + 128], sca=expc[:, h:h + 1]:
                       s.activation(o, i_, AF.Copy, scale=sca)),
                 reads=[("pb", bank2), "expc"], writes=[("itmp", h)])
        itmp_keys = [("itmp", h) for h in range(8)]
        for bank in (2, 3):
            b = bank - 2
            K.ev("v", (lambda v, o=y0[:, 512 * b:512 * b + 512],
                       i0=K.pb[bank][:, 0:512], i1=itmp[:, 512 * b:512 * b + 512]:
                       v.tensor_tensor(o, i0, i1, ALU.add)),
                 reads=[("pb", bank)] + itmp_keys, writes=[("y0", b)])
            ogb = bass.AP(og, 4 * b, [[8, 128], [1, 4], [0, 128]])
            K.ev("v", (lambda v,
                       o=y_bf[:, 512 * b:512 * b + 512].rearrange("p (c w) -> p c w", w=128),
                       i0=y0[:, 512 * b:512 * b + 512].rearrange("p (c w) -> p c w", w=128),
                       i1=ogb: v.tensor_tensor(o, i0, i1, ALU.mult)),
                 reads=[("y0", b), "og"], writes=[("y_bf", b)])
        ybf_keys = [("y_bf", 0), ("y_bf", 1)]

        wrep = bass.AP(wcol, 0, [[8, 128], [1, 8], [0, 128]])
        K.ev("v", (lambda v, o=khat[:, :].rearrange("p (c w) -> p c w", w=128),
                   i0=k_ro[:, :].rearrange("p (c w) -> p c w", w=128), i1=wrep:
                   v.tensor_tensor(o, i0, i1, ALU.mult)),
             reads=["k_ro", "wcol"], writes=["khat"])
        for h in range(8):
            bank = h // 4
            c0 = 128 * (h % 4)
            K.mm_group(bank, c0, c0 + 128,
                       [(khat[:, 128 * h:128 * h + 128],
                         v_bf[:, 128 * h:128 * h + 128])], ["khat", "v_bf"])
        for h in range(8):
            bank = h // 4
            c0 = 128 * (h % 4)
            K.ev("v", (lambda v, o=W["M_sb"][:, h, :], i0=W["M_sb"][:, h, :],
                       sca=decC[:, h:h + 1], i1=K.pb[bank][:, c0:c0 + 128]:
                       v.scalar_tensor_tensor(o, i0, sca, i1, ALU.mult, ALU.add)),
                 reads=[("pb", bank), "M_sb", "decC"], writes=["M_sb"])

        for half in range(2):
            bb = half
            for q in range(4):
                bc = 4 * half + q
                K.transpose(bb, 128 * q, y_bf[:, 128 * bc:128 * bc + 128], ybf_keys)
            K.ev("s", (lambda s, o=yfm2[:, 4 * half:4 * half + 4, :],
                       i_=K.pbf[bb][:, 0:512]: s.copy(o, i_)),
                 reads=[("pbf", bb)], writes=["yfm2"])
        for bank, eg in ((2, 0), (3, 1)):
            terms = [(yfm2[:, kc, :], W["wout1T"][:, kc, 512 * eg:512 * eg + 512])
                     for kc in range(8)]
            terms.append((K.C["ones_bf"][0:1, 0:128],
                          W["bout"][0:1, 512 * eg:512 * eg + 512]))
            K.mm_group(bank, 0, 512, terms,
                       [("wout1T", kc) for kc in range(8)] + ["yfm2", "bout",
                                                             "ones_bf"])
            b = bank - 2
            K.ev("v", (lambda v, o=s1t[:, 512 * b:512 * b + 512],
                       i_=K.pb[bank][:, 0:512]: v.tensor_copy(o, i_)),
                 reads=[("pb", bank)], writes=[("s1t", b)])
        K.dma("p", K.io["s1"][rows, :], s1t[:, :],
              reads=[("s1t", 0), ("s1t", 1)], writes=[("s1_h", ci)])

        L = lg
        K.ev("v", (lambda v, o=L[:, 8:9], i_=L[:, 0:8]:
                   v.reduce_max(o, i_, axis=AX.X, negate=True)),
             reads=[("lg", "logits")], writes=[("lg", "nm")])
        K.ev("s", (lambda s, o=L[:, 16:24], i_=L[:, 0:8], b=L[:, 8:9]:
                   s.activation(o, i_, AF.Exp, bias=b)),
             reads=[("lg", "logits"), ("lg", "nm")], writes=[("lg", "ex")])
        K.ev("v", (lambda v, o=L[:, 9:10], i_=L[:, 16:24]:
                   v.reduce_sum(o, i_, axis=AX.X)),
             reads=[("lg", "ex")], writes=[("lg", "sum")])
        K.ev("v", (lambda v, o=L[:, 10:11], i_=L[:, 9:10]: v.reciprocal(o, i_)),
             reads=[("lg", "sum")], writes=[("lg", "rs")])
        K.ev("v", (lambda v, o=L[:, 16:24], i_=L[:, 16:24], s1=L[:, 10:11]:
                   v.tensor_scalar_mul(o, i_, s1)),
             reads=[("lg", "ex"), ("lg", "rs")], writes=[("lg", "sco")])
        K.ev("v", (lambda v, o=L[:, 24:32], a=L[:, 16:24], b=W["gb_bc"][:, :]:
                   v.tensor_add(o, a, b)),
             reads=[("lg", "sco"), "gb_bc"], writes=[("lg", "bi")])
        K.ev("v", (lambda v, o=L[:, 11:12], i_=L[:, 24:32]:
                   v.reduce_max(o, i_, axis=AX.X)),
             reads=[("lg", "bi")], writes=[("lg", "m1")])
        K.ev("v", (lambda v, o=L[:, 32:40], i_=L[:, 24:32], s1=L[:, 11:12]:
                   v.tensor_scalar(o, i_, s1, None, ALU.is_ge)),
             reads=[("lg", "bi"), ("lg", "m1")], writes=[("lg", "eq")])
        K.ev("v", (lambda v, o=L[:, 40:48], i0=L[:, 32:40], i1=idxp[:, :]:
                   v.scalar_tensor_tensor(o, i0, -1.0e4, i1, ALU.mult, ALU.add)),
             reads=[("lg", "eq"), "idxp"], writes=[("lg", "sel")])
        K.ev("v", (lambda v, o=L[:, 12:13], i_=L[:, 40:48]:
                   v.tensor_reduce(o, i_, axis=AX.X, op=ALU.min)),
             reads=[("lg", "sel")], writes=[("lg", "i1")])
        K.ev("v", (lambda v, o=L[:, 32:40], i_=idxc[:, :], s1=L[:, 12:13]:
                   v.tensor_scalar(o, i_, s1, None, ALU.is_equal)),
             reads=["idxc", ("lg", "i1")], writes=[("lg", "mk1")])
        K.ev("v", (lambda v, o=L[:, 24:32], i0=L[:, 32:40], i1=L[:, 24:32]:
                   v.scalar_tensor_tensor(o, i0, -1.0e30, i1, ALU.mult, ALU.add)),
             reads=[("lg", "mk1"), ("lg", "bi")], writes=[("lg", "bi2")])
        K.ev("v", (lambda v, o=L[:, 13:14], i_=L[:, 24:32]:
                   v.reduce_max(o, i_, axis=AX.X)),
             reads=[("lg", "bi2")], writes=[("lg", "m2")])
        K.ev("v", (lambda v, o=L[:, 40:48], i_=L[:, 24:32], s1=L[:, 13:14]:
                   v.tensor_scalar(o, i_, s1, None, ALU.is_ge)),
             reads=[("lg", "bi2"), ("lg", "m2")], writes=[("lg", "eq2")])
        K.ev("v", (lambda v, o=L[:, 40:48], i0=L[:, 40:48], i1=idxp[:, :]:
                   v.scalar_tensor_tensor(o, i0, -1.0e4, i1, ALU.mult, ALU.add)),
             reads=[("lg", "eq2"), "idxp"], writes=[("lg", "sel2")])
        K.ev("v", (lambda v, o=L[:, 14:15], i_=L[:, 40:48]:
                   v.tensor_reduce(o, i_, axis=AX.X, op=ALU.min)),
             reads=[("lg", "sel2")], writes=[("lg", "i2")])
        K.ev("v", (lambda v, o=L[:, 40:48], i_=idxc[:, :], s1=L[:, 14:15]:
                   v.tensor_scalar(o, i_, s1, None, ALU.is_equal)),
             reads=["idxc", ("lg", "i2")], writes=[("lg", "mk2")])
        K.ev("v", (lambda v, o=L[:, 32:40], a=L[:, 32:40], b=L[:, 40:48]:
                   v.tensor_add(o, a, b)),
             reads=[("lg", "mk1"), ("lg", "mk2")], writes=[("lg", "mk")])
        K.ev("v", (lambda v, o=cwt[:, :], a=L[:, 32:40], b=L[:, 16:24]:
                   v.tensor_mul(o, a, b)),
             reads=[("lg", "mk"), ("lg", "sco")], writes=["cwt"])
        if DEBUG and ci == DBG_CI:
            K.dbg("lg", lg[:, :], [128, 48], F32,
                  [("lg", "logits"), ("lg", "sco"), ("lg", "mk")])
            K.dbg("cwt", cwt[:, :], [128, 8], F32, ["cwt"])
            K.dbg("zg", zg[:, :], [128, 24], F32, ["zg"])
            K.dbg("nl", nl[:, :], [128, 16], F32, ["nl"])
            K.dbg("cum", cum[:, :], [128, 8], F32, ["cum"])
            K.dbg("og2", og[:, :], [128, 8], F32, ["og"])
            K.dbg("rsc", sc[:, :], [128, 8], F32, [("sc", 2), ("sc", 3)])
            K.dbg("gbbc", W["gb_bc"][:, :], [128, 8], F32, ["gb_bc"])
            K.dbg("idxp", idxp[:, :], [128, 8], F32, ["idxp"])
            K.dbg("idxc", idxc[:, :], [128, 8], F32, ["idxc"])
        K.dma("p", K.io["cwb"][rows, :], cwt[:, :], reads=["cwt"],
              writes=[("cw_h", ci)])

        if moe_casts:
            while moe_casts and len(moe_casts) > (nch - 1 - ci) * 2:
                moe_casts.pop(0)()

    P.barrier()
    ar.release()


def decl_io_A(nc):
    io = {}
    io["x"] = nc.dram_tensor("x", [T, D], F32, kind="ExternalInput").ap()
    for nm, hbm in (("wq0", "L0_attn_Wq"), ("wk0", "L0_attn_Wk"),
                    ("wv0", "L0_attn_Wv"), ("wo0", "L0_attn_Wo")):
        io[nm] = nc.dram_tensor(hbm, [1024, 1024], F32, kind="ExternalInput").ap()
    io["wfr"] = nc.dram_tensor("L0_ffn_w_freq_real", [1024, 1024], F32,
                               kind="ExternalInput").ap()
    io["wfi"] = nc.dram_tensor("L0_ffn_w_freq_imag", [1024, 1024], F32,
                               kind="ExternalInput").ap()
    io["fbias"] = nc.dram_tensor("L0_ffn_bias", [1024], F32, kind="ExternalInput").ap()
    for nm in ("c1", "c2", "c2n", "ut01", "id_bf"):
        shape, dt = CONST_SPECS[nm]
        io[nm] = nc.dram_tensor(nm, shape, dt, kind="ExternalInput").ap()
    return io


def build_A(outputs_external=True, nch=NCH):
    nc = bass.Bass("TRN2", target_bir_lowering=False)
    K = Ker(nc)
    K.io = decl_io_A(nc)
    kind = "ExternalOutput" if outputs_external else "Internal"
    K.io["x1"] = nc.dram_tensor("x1", [T, D], F32, kind=kind).ap()
    K.io["xn1f"] = nc.dram_tensor("xn1f", [T, D], F32, kind=kind).ap()
    K.io["xn1b"] = nc.dram_tensor("xn1b", [T, D], BF16, kind=kind).ap()
    W = phase0_l0(K)
    phase1(K, W, moe_casts=None, nch=nch)
    K.P.build()
    return nc




def decl_io_B(nc):
    io = {}
    io["xn1f"] = nc.dram_tensor("xn1f", [T, D], F32, kind="ExternalInput").ap()
    for nm, hbm, shape in (
            ("wq1", "L1_attn_Wq", [1024, 1024]), ("wk1", "L1_attn_Wk", [1024, 1024]),
            ("wv1", "L1_attn_Wv", [1024, 1024]), ("wout1", "L1_attn_Wout", [1024, 1024]),
            ("wbeta", "L1_attn_Wbeta", [8, 1024]), ("wig", "L1_attn_Wig", [8, 1024]),
            ("wog", "L1_attn_Wog", [8, 1024]), ("gate_w", "L1_ffn_gate_w", [8, 1024]),
            ("norm_w", "L1_attn_norm_w", [1024]), ("bbeta", "L1_attn_bbeta", [8]),
            ("big", "L1_attn_big", [8]), ("bog", "L1_attn_bog", [8]),
            ("bout", "L1_attn_bout", [1024]), ("gate_b", "L1_ffn_gate_b", [8])):
        io[nm] = nc.dram_tensor(hbm, shape, F32, kind="ExternalInput").ap()
    for nm in ("cumm", "maskd", "id_f32", "id_bf", "ropecos", "ropesin"):
        shape, dt = CONST_SPECS[nm]
        io[nm] = nc.dram_tensor(nm, shape, dt, kind="ExternalInput").ap()
    return io


def build_B(nch=NCH):
    nc = bass.Bass("TRN2", target_bir_lowering=False)
    K = Ker(nc)
    K.io = decl_io_B(nc)
    K.io["s1"] = nc.dram_tensor("s1", [T, D], F32, kind="ExternalOutput").ap()
    K.io["cwb"] = nc.dram_tensor("cwb", [T, 8], F32, kind="ExternalOutput").ap()
    W = phase2_prep(K)
    phase2(K, W, moe_casts=None, nch=nch)
    K.P.build()
    return nc





def phase4(K, run_casts=False):
    nc, P, ar = K.nc, K.P, K.ar
    if run_casts or not hasattr(K, "moe_cache"):
        thunks = moe_cast_list(K)
        for th in thunks:
            th()
    w1c, w3c, w2c = K.moe_cache["w1c"], K.moe_cache["w3c"], K.moe_cache["w2c"]
    ar.mark()
    if "ones_bf" not in K.C:
        ones_bf = ar.alloc("ones_bf", [1, 512], BF16)
        K.C["ones_bf"] = ones_bf
        K.ev("p", lambda g: g.memset(ones_bf[:, :], 1.0), writes=("ones_bf",))
    if "ones_f32" not in K.C:
        ones_f32 = ar.alloc("ones_f32", [128, 128], F32)
        K.C["ones_f32"] = ones_f32
        K.ev("p", lambda g: g.memset(ones_f32[:, :], 1.0), writes=("ones_f32",))
    ONESF = K.C["ones_f32"]

    bcols = ar.alloc("bcols", [128, 9, 32], F32)
    ar.mark()
    bstg = ar.alloc("bstg", [1, 2048], F32)
    brow = ar.alloc("brow", [1, 2048], F32)
    for e in range(9):
        for t, src in ((0, "b1"), (1, "b3")):
            if e < 8:
                row_ap = K.io[src][e, :].rearrange("(a d) -> a d", a=1)
            else:
                row_ap = K.io["s" + src].rearrange("(a d) -> a d", a=1)
            K.dma("y", bstg[:, :], row_ap, writes=["bstg"])
            K.ev("v", (lambda v, o=brow[:, 0:1024],
                       i_=bass.AP(bstg, 0, [[2048, 1], [2, 1024]]):
                       v.tensor_copy(o, i_)),
                 reads=["bstg"], writes=[("brow", 0)])
            K.ev("v", (lambda v, o=brow[:, 1024:2048],
                       i_=bass.AP(bstg, 1, [[2048, 1], [2, 1024]]):
                       v.tensor_copy(o, i_)),
                 reads=["bstg"], writes=[("brow", 1)])
            for half in range(2):
                for fc in range(8):
                    col = (2 * t + half) * 8 + fc
                    K.mm_group_ap(K.pb[0][:, col:col + 1], ("pb", 0),
                                  [(brow[0:1, 1024 * half + 128 * fc:
                                         1024 * half + 128 * fc + 128],
                                    ONESF[0:1, 0:1])],
                                  [("brow", half), "ones_f32"])
            K.ev("v", (lambda v, o=bcols[:, e, 16 * t:16 * t + 8],
                       i_=K.pb[0][:, 16 * t:16 * t + 8]: v.tensor_copy(o, i_)),
                 reads=[("pb", 0)], writes=[("bcols", e, t, 0)])
            K.ev("v", (lambda v, o=bass.AP(bcols, 32 * e + 8 + 16 * t,
                                           [[288, 128], [1, 8]]),
                       i_=K.pb[0][:, 8 + 16 * t:16 + 16 * t]: v.tensor_copy(o, i_)),
                 reads=[("pb", 0)], writes=[("bcols", e, t, 1)])
    bcol_keys = [("bcols", e, t, hh) for e in range(9) for t in range(2)
                 for hh in range(2)]
    P.barrier()
    ar.release()
    b2r = [ar.alloc(f"b2r{i}", [1, 1024], BF16) for i in range(2)]

    xn1fm = ar.alloc("xn1fm", [128, 8, 1024], BF16)
    cwcol = ar.alloc("cwcol", [128, 64], F32)
    acc = ar.alloc("acc", [128, 8, 1024], F32)
    gsig = ar.alloc("gsig", [128, 8, 1024], BF16)
    abuf = ar.alloc("p4a", [128, 8, 1024], BF16)
    wb = [ar.alloc(f"wb{i}", [128, 8, 1024], BF16) for i in range(4)]
    wb2 = ar.alloc("wb2", [128, 8, 1024], BF16)
    tb1 = ar.alloc("tb1", [128, 1024], F32)
    tb2 = ar.alloc("tb2", [128, 1024], F32)
    tb3 = ar.alloc("tb3", [128, 1024], F32)
    xf = [ar.alloc(f"p4x{i}", [128, 1024], F32) for i in range(2)]
    sf = [ar.alloc(f"p4s{i}", [128, 1024], F32) for i in range(2)]

    def load_h(slot, e, half, src, q):
        for dc in range(8):
            K.dma_t(q, wb[slot][:, dc, :],
                    src[e, 1024 * half:1024 * half + 1024,
                        128 * dc:128 * dc + 128],
                    reads=[(("w1c" if src is w1c else "w3c"), e, half)],
                    writes=[("wb", slot, dc)])
        return [("wb", slot, dc) for dc in range(8)]

    def load_w2(e, q):
        for dc in range(8):
            K.dma_t(q, wb2[:, dc, :], w2c[e, 0:1024, 128 * dc:128 * dc + 128],
                    reads=[("w2c", e)], writes=[("wb2", dc)])
        return [("wb2", dc) for dc in range(8)]

    NEXP = 9
    for p in range(NPASS):
        rows = slice(PASS_TOK * p, PASS_TOK * (p + 1))
        for dc in range(8):
            K.dma_t("y", xn1fm[:, dc, :],
                    K.io["xn1b"][rows, 128 * dc:128 * dc + 128],
                    reads=["xn1b_all"], writes=[("xn1fm", dc)])
        xn_keys = [("xn1fm", dc) for dc in range(8)]
        K.dma("y", cwcol[:, :].rearrange("p (tt e) -> p tt e", e=8),
              K.io["cwb"][rows, :].rearrange("(tt q) e -> q tt e", q=128),
              reads=["cwb_all"], writes=["cwcol"])
        K.ev("v", lambda v, o=acc[:, :, :]: v.memset(o, 0.0), writes=["acc"])

        for e in range(NEXP):
            sl_e = (0, 1)
            sl_o = (2, 3)
            if e == 0 and p == 0:
                k1e = load_h(sl_e[0], 0, 0, w1c, "y")
                k3e = load_h(sl_e[1], 0, 0, w3c, "y")
            else:
                k1e = [("wb", sl_e[0], dc) for dc in range(8)]
                k3e = [("wb", sl_e[1], dc) for dc in range(8)]
            k1o = load_h(sl_o[0], e, 1, w1c, "y")
            k3o = load_h(sl_o[1], e, 1, w3c, "y")
            kw2 = load_w2(e, "y")
            b2src = K.io["b2"][e, :] if e < 8 else K.io["sb2"]
            K.dma("p", b2r[e % 2][:, :], b2src.rearrange("(a d) -> a d", a=1),
                  writes=[("b2r", e % 2)])

            for fe in range(8):
                for tg in range(2):
                    b_h1 = tg
                    b_h3 = 2 + tg
                    terms1 = [(wb[sl_e[0]][:, dc, 128 * fe:128 * fe + 128],
                               xn1fm[:, dc, 512 * tg:512 * tg + 512])
                              for dc in range(8)]
                    K.mm_group(b_h1, 0, 512, terms1, k1e + xn_keys)
                    terms3 = [(wb[sl_e[1]][:, dc, 128 * fe:128 * fe + 128],
                               xn1fm[:, dc, 512 * tg:512 * tg + 512])
                              for dc in range(8)]
                    K.mm_group(b_h3, 0, 512, terms3, k3e + xn_keys)
                    bc1 = bcols[:, e, 0 * 8 + fe:0 * 8 + fe + 1]
                    bc3 = bcols[:, e, 16 + fe:16 + fe + 1]
                    t1s = tb1[:, 512 * tg:512 * tg + 512]
                    t2s = tb2[:, 512 * tg:512 * tg + 512]
                    t3s = tb3[:, 512 * tg:512 * tg + 512]
                    K.ev("s", (lambda s, o=t1s, i_=K.pb[b_h1][:, 0:512],
                               b=bc1: s.activation(o, i_, AF.Identity, bias=b)),
                         reads=[("pb", b_h1)] + bcol_keys, writes=[("tb1", tg)])
                    K.ev("v", (lambda v, o=t2s, i0=K.pb[b_h3][:, 0:512],
                               sca=bc3, i1=t1s:
                               v.scalar_tensor_tensor(o, i0, sca, i1, ALU.add,
                                                      ALU.mult)),
                         reads=[("pb", b_h3), ("tb1", tg)] + bcol_keys,
                         writes=[("tb2", tg)])
                    K.ev("v", (lambda v, o=t2s, i_=t2s:
                               v.tensor_scalar_min(o, i_, 7.0)),
                         reads=[("tb2", tg)], writes=[("tb2", tg)])
                    K.ev("s", (lambda s, o=t3s, i_=t2s:
                               s.activation(o, i_, AF.Sigmoid,
                                            scale=np.float32(1.702))),
                         reads=[("tb2", tg)], writes=[("tb3", tg)])
                    K.ev("v", (lambda v, o=gsig[:, fe, 512 * tg:512 * tg + 512],
                               a=t2s, b=t3s:
                               v.tensor_mul(o, a, b)),
                         reads=[("tb2", tg), ("tb3", tg)], writes=[("gsig", fe, tg)])
            gs_keys = [("gsig", fe, tg) for fe in range(8) for tg in range(2)]

            for fe in range(8):
                for tg in range(2):
                    b_h1 = tg
                    b_h3 = 2 + tg
                    terms1 = [(wb[sl_o[0]][:, dc, 128 * fe:128 * fe + 128],
                               xn1fm[:, dc, 512 * tg:512 * tg + 512])
                              for dc in range(8)]
                    K.mm_group(b_h1, 0, 512, terms1, k1o + xn_keys)
                    terms3 = [(wb[sl_o[1]][:, dc, 128 * fe:128 * fe + 128],
                               xn1fm[:, dc, 512 * tg:512 * tg + 512])
                              for dc in range(8)]
                    K.mm_group(b_h3, 0, 512, terms3, k3o + xn_keys)
                    bc1 = bcols[:, e, 8 + fe:8 + fe + 1]
                    bc3 = bcols[:, e, 24 + fe:24 + fe + 1]
                    t1s = tb1[:, 512 * tg:512 * tg + 512]
                    t2s = tb2[:, 512 * tg:512 * tg + 512]
                    K.ev("s", (lambda s, o=t1s, i_=K.pb[b_h1][:, 0:512],
                               b=bc1: s.activation(o, i_, AF.Identity, bias=b)),
                         reads=[("pb", b_h1)] + bcol_keys, writes=[("tb1", tg)])
                    K.ev("v", (lambda v, o=t2s, i0=K.pb[b_h3][:, 0:512],
                               sca=bc3, i1=t1s:
                               v.scalar_tensor_tensor(o, i0, sca, i1, ALU.add,
                                                      ALU.mult)),
                         reads=[("pb", b_h3), ("tb1", tg)] + bcol_keys,
                         writes=[("tb2", tg)])
                    K.ev("v", (lambda v, o=t2s, i_=t2s:
                               v.tensor_scalar(o, i_, 7.0, -7.0, ALU.min, ALU.max)),
                         reads=[("tb2", tg)], writes=[("tb2", tg)])
                    K.ev("v", (lambda v, o=abuf[:, fe, 512 * tg:512 * tg + 512],
                               i0=t2s,
                               i1=gsig[:, fe, 512 * tg:512 * tg + 512]:
                               v.scalar_tensor_tensor(o, i0, 1.0, i1, ALU.add,
                                                      ALU.mult)),
                         reads=[("tb2", tg), ("gsig", fe, tg)],
                         writes=[("p4a", fe, tg)])
            a_keys = [("p4a", fe, tg) for fe in range(8) for tg in range(2)]
            if DEBUG and p == 0 and e == 0:
                K.dbg("gsig0", gsig[:, :, :], [128, 8, 1024], BF16, gs_keys)
                K.dbg("a0", abuf[:, :, :], [128, 8, 1024], BF16, a_keys)
                K.dbg("bcols", bcols[:, :, :], [128, 9, 32], F32, bcol_keys)
                K.dbg("xnfm0", xn1fm[:, :, :], [128, 8, 1024], BF16, xn_keys)
                K.dbg("cwcol0", cwcol[:, :], [128, 64], F32, ["cwcol"])
            if e + 1 < NEXP:
                load_h(0, e + 1, 0, w1c, "y")
                load_h(1, e + 1, 0, w3c, "y")
            elif p + 1 < NPASS:
                load_h(0, 0, 0, w1c, "y")
                load_h(1, 0, 0, w3c, "y")

            for tt in range(8):
                for dh in range(2):
                    bank = 4 + dh
                    terms = [(abuf[:, fp, 128 * tt:128 * tt + 128],
                              wb2[:, fp, 512 * dh:512 * dh + 512])
                             for fp in range(8)]
                    terms.append((K.C["ones_bf"][0:1, 0:128],
                                  b2r[e % 2][0:1, 512 * dh:512 * dh + 512]))
                    K.mm_group(bank, 0, 512, terms,
                               a_keys + kw2 + ["ones_bf", ("b2r", e % 2)])
                    if e < 8:
                        cwc = cwcol[:, 8 * tt + e:8 * tt + e + 1]
                        K.ev("s", (lambda s, o=tb1[:, 512 * dh:512 * dh + 512],
                                   i_=K.pb[bank][:, 0:512], sca=cwc:
                                   s.activation(o, i_, AF.Copy, scale=sca)),
                             reads=[("pb", bank), "cwcol"], writes=[("tb1", dh)])
                    else:
                        K.ev("s", (lambda s, o=tb1[:, 512 * dh:512 * dh + 512],
                                   i_=K.pb[bank][:, 0:512]: s.copy(o, i_)),
                             reads=[("pb", bank)], writes=[("tb1", dh)])
                    K.ev("v", (lambda v, o=acc[:, tt, 512 * dh:512 * dh + 512],
                               a=acc[:, tt, 512 * dh:512 * dh + 512],
                               b=tb1[:, 512 * dh:512 * dh + 512]:
                               v.tensor_add(o, a, b)),
                         reads=[("tb1", dh), "acc"], writes=["acc"])

        if DEBUG and p == 0:
            K.dbg("accp0", acc[:, :, :], [128, 8, 1024], F32, ["acc"])
        for tt in range(8):
            par = tt % 2
            trows = slice(PASS_TOK * p + 128 * tt, PASS_TOK * p + 128 * tt + 128)
            K.dma("y", xf[par][:, :], K.io["x1"][trows, :],
                  reads=["x1_all"], writes=[(f"p4x{par}",)])
            K.dma("s", sf[par][:, :], K.io["s1"][trows, :],
                  reads=["s1_all"], writes=[(f"p4s{par}",)])
            K.ev("s", (lambda s, o=tb1[:, :], i_=acc[:, tt, :]:
                       s.activation(o, i_, AF.Sigmoid)),
                 reads=["acc"], writes=[("tb1", 0), ("tb1", 1)])
            K.ev("v", (lambda v, o=tb2[:, :], a=acc[:, tt, :], b=tb1[:, :]:
                       v.tensor_mul(o, a, b)),
                 reads=["acc", ("tb1", 0), ("tb1", 1)], writes=[("tb2", 0), ("tb2", 1)])
            K.ev("v", (lambda v, o=tb1[:, :], a=sf[par][:, :], b=tb2[:, :]:
                       v.tensor_mul(o, a, b)),
                 reads=[(f"p4s{par}",), ("tb2", 0), ("tb2", 1)],
                 writes=[("tb1", 0), ("tb1", 1)])
            K.ev("v", (lambda v, o=xf[par][:, :], a=xf[par][:, :], b=tb1[:, :]:
                       v.tensor_add(o, a, b)),
                 reads=[(f"p4x{par}",), ("tb1", 0), ("tb1", 1)],
                 writes=[(f"p4x{par}",)])
            K.dma("p", K.io["out"][trows, :], xf[par][:, :],
                  reads=[(f"p4x{par}",)], writes=[("out_h", p, tt)])

    P.barrier()
    ar.release()


def decl_io_C(nc):
    io = {}
    io["x1"] = nc.dram_tensor("x1", [T, D], F32, kind="ExternalInput").ap()
    io["s1"] = nc.dram_tensor("s1", [T, D], F32, kind="ExternalInput").ap()
    io["xn1b"] = nc.dram_tensor("xn1b", [T, D], BF16, kind="ExternalInput").ap()
    io["cwb"] = nc.dram_tensor("cwb", [T, 8], F32, kind="ExternalInput").ap()
    for nm, hbm, shape in (
            ("w1", "L1_ffn_w1", [8, 2048, 1024]), ("w3", "L1_ffn_w3", [8, 2048, 1024]),
            ("w2", "L1_ffn_w2", [8, 1024, 1024]), ("b1", "L1_ffn_b1", [8, 2048]),
            ("b3", "L1_ffn_b3", [8, 2048]), ("b2", "L1_ffn_b2", [8, 1024]),
            ("sw1", "L1_ffn_sw1", [2048, 1024]), ("sw3", "L1_ffn_sw3", [2048, 1024]),
            ("sw2", "L1_ffn_sw2", [1024, 1024]), ("sb1", "L1_ffn_sb1", [2048]),
            ("sb3", "L1_ffn_sb3", [2048]), ("sb2", "L1_ffn_sb2", [1024])):
        io[nm] = nc.dram_tensor(hbm, shape, F32, kind="ExternalInput").ap()
    return io


def build_C():
    nc = bass.Bass("TRN2", target_bir_lowering=False)
    K = Ker(nc)
    K.io = decl_io_C(nc)
    K.io["out"] = nc.dram_tensor("out", [T, D], F32, kind="ExternalOutput").ap()
    phase4(K, run_casts=True)
    K.P.build()
    return nc


_CONSTS = None


def _consts():
    global _CONSTS
    if _CONSTS is None:
        _CONSTS = host_constants()
    return _CONSTS


def kernel(**inputs):
    consts = _consts()
    x_full = inputs["L0_attn_x"]
    cores = list(range(NCORES))

    ncA = build_A()
    imA = []
    for c in cores:
        m = {"x": np.ascontiguousarray(x_full[c])}
        for nm in ("L0_attn_Wq", "L0_attn_Wk", "L0_attn_Wv", "L0_attn_Wo",
                   "L0_ffn_w_freq_real", "L0_ffn_w_freq_imag", "L0_ffn_bias"):
            m[nm] = inputs[nm]
        for nm in ("c1", "c2", "c2n", "ut01", "id_bf"):
            m[nm] = consts[nm]
        imA.append(m)
    resA = run_bass_kernel_spmd(ncA, imA, core_ids=cores)

    ncB = build_B()
    imB = []
    for c in cores:
        m = {"xn1f": resA.results[c]["xn1f"]}
        for nm in ("L1_attn_Wq", "L1_attn_Wk", "L1_attn_Wv", "L1_attn_Wout",
                   "L1_attn_Wbeta", "L1_attn_Wig", "L1_attn_Wog",
                   "L1_ffn_gate_w", "L1_attn_norm_w", "L1_attn_bbeta",
                   "L1_attn_big", "L1_attn_bog", "L1_attn_bout", "L1_ffn_gate_b"):
            m[nm] = inputs[nm]
        for nm in ("cumm", "maskd", "id_f32", "id_bf", "ropecos", "ropesin"):
            m[nm] = consts[nm]
        imB.append(m)
    resB = run_bass_kernel_spmd(ncB, imB, core_ids=cores)

    ncC = build_C()
    imC = []
    for c in cores:
        m = {"x1": resA.results[c]["x1"], "s1": resB.results[c]["s1"],
             "xn1b": resA.results[c]["xn1b"], "cwb": resB.results[c]["cwb"]}
        for nm in ("L1_ffn_w1", "L1_ffn_w3", "L1_ffn_w2", "L1_ffn_b1",
                   "L1_ffn_b3", "L1_ffn_b2", "L1_ffn_sw1", "L1_ffn_sw3",
                   "L1_ffn_sw2", "L1_ffn_sb1", "L1_ffn_sb3", "L1_ffn_sb2"):
            m[nm] = inputs[nm]
        imC.append(m)
    resC = run_bass_kernel_spmd(ncC, imC, core_ids=cores)

    return np.stack([resC.results[c]["out"] for c in cores], axis=0)


# version 05
# speedup vs baseline: 1.0438x; 1.0438x over previous
import contextlib
import numpy as np
import ml_dtypes
import concourse.bass as bass
import concourse.mybir as mybir
from concourse.bass_utils import run_bass_kernel_spmd

F32 = mybir.dt.float32
BF16 = mybir.dt.bfloat16
AF = mybir.ActivationFunctionType
ALU = mybir.AluOpType
AX = mybir.AxisListType

NCORES = 8
T = 4096
D = 1024
CH = 128
NCH = T // CH
PASS_TOK = 1024
NPASS = T // PASS_TOK
LNSCALE = float(np.log(1.0 / np.sqrt(128.0)))

ENGS = ("t", "v", "s", "p", "y")



class Op:
    __slots__ = ("eng", "fn", "deps", "dma", "idx", "reads", "writes")

    def __init__(self, eng, fn, deps, dma, idx, reads=(), writes=()):
        self.eng, self.fn, self.deps, self.dma, self.idx = eng, fn, deps, dma, idx
        self.reads, self.writes = tuple(reads), tuple(writes)


DEDUP = True


class Prog:

    def __init__(self, nc):
        self.nc = nc
        self.ops = []
        self.eng_ops = {e: [] for e in ENGS}
        self.writer = {}
        self.readers = {}
        self.floor = set()

    def emit(self, eng, fn, reads=(), writes=(), dma=False):
        idx = len(self.ops)
        deps = set(self.floor)
        for k in reads:
            w = self.writer.get(k)
            if w is not None:
                deps.add(w)
        for k in writes:
            w = self.writer.get(k)
            if w is not None:
                deps.add(w)
            for r in self.readers.get(k, ()):
                deps.add(r)
        deps.discard(idx)
        op = Op(eng, fn, deps, dma, idx, reads, writes)
        self.ops.append(op)
        self.eng_ops[eng].append(idx)
        for k in reads:
            self.readers.setdefault(k, []).append(idx)
        for k in writes:
            self.writer[k] = idx
            self.readers[k] = [idx]
        return idx

    def barrier(self):
        lasts = set()
        for e in ENGS:
            if self.eng_ops[e]:
                lasts.add(self.eng_ops[e][-1])
        self.floor = lasts

    def build(self, final_eng="p"):
        nc = self.nc
        ops = self.ops
        NSLOT = 8
        need = [False] * len(ops)
        dep_targets = set()
        for op in ops:
            for d in op.deps:
                dop = ops[d]
                if dop.dma or dop.eng != op.eng or op.eng != "t":
                    need[d] = True
                dep_targets.add(d)
        tail = []
        for i, op in enumerate(ops):
            if op.dma and i not in dep_targets:
                need[i] = True
                tail.append(i)
        stack = contextlib.ExitStack()
        sem_c = {e: stack.enter_context(nc.semaphore(f"sc{e}")) for e in ENGS}
        sem_d = {e: [stack.enter_context(nc.semaphore(f"sd{e}{i}"))
                     for i in range(NSLOT)] for e in ENGS}
        tok = [None] * len(ops)
        pre_wait = [None] * len(ops)
        cnt_c = {e: 0 for e in ENGS}
        dma_i = {e: 0 for e in ENGS}
        slot_uses = {e: [0] * NSLOT for e in ENGS}
        for op in ops:
            if op.dma:
                s = dma_i[op.eng] % NSLOT
                dma_i[op.eng] += 1
                u = slot_uses[op.eng][s]
                if u > 0:
                    pre_wait[op.idx] = (sem_d[op.eng][s], 16 * u)
                slot_uses[op.eng][s] = u + 1
                tok[op.idx] = (sem_d[op.eng][s], 16 * (u + 1))
            elif need[op.idx]:
                cnt_c[op.eng] += 1
                tok[op.idx] = (sem_c[op.eng], cnt_c[op.eng])
        waits = [None] * len(ops)
        hi = {}
        for op in ops:
            wl = []
            if pre_wait[op.idx] is not None:
                sem, v = pre_wait[op.idx]
                key = (op.eng, id(sem))
                if hi.get(key, 0) < v:
                    hi[key] = v
                    wl.append((sem, v))
            for d in sorted(op.deps):
                dop = ops[d]
                if (not dop.dma) and dop.eng == op.eng and op.eng == "t":
                    continue
                sem, v = tok[d]
                key = (op.eng, id(sem))
                if DEDUP and hi.get(key, 0) >= v:
                    continue
                hi[key] = max(hi.get(key, 0), v)
                wl.append((sem, v))
            waits[op.idx] = wl
        final_waits = []
        fhi = {}
        for i in sorted(tail):
            sem, v = tok[i]
            key = id(sem)
            if fhi.get(key, 0) >= v:
                continue
            fhi[key] = v
            final_waits.append((sem, v))

        with nc.Block() as block:
            def stream(e):
                def body(eng):
                    for i in self.eng_ops[e]:
                        for (sem, v) in waits[i]:
                            eng.wait_ge(sem, v)
                        inst = ops[i].fn(eng)
                        if ops[i].dma:
                            inst.then_inc(tok[i][0], 16)
                        elif tok[i] is not None:
                            inst.then_inc(tok[i][0], 1)
                    if e == final_eng:
                        for (sem, v) in final_waits:
                            eng.wait_ge(sem, v)
                return body
            block.tensor(stream("t"))
            block.vector(stream("v"))
            block.scalar(stream("s"))
            block.gpsimd(stream("p"))
            block.sync(stream("y"))
        self._semstack = stack
        return self


class Arena:
    def __init__(self, nc, margin=6 * 1024):
        self.nc = nc
        size = nc.sbuf_bytes_remaining - margin
        self.backing = nc.alloc_sbuf_tensor("arena0", [128, size], mybir.dt.uint8)
        base = nc.lookup_mloc(self.backing).addr
        self.off = base
        self.limit = base + size
        self.marks = []
        self.n = 0

    def alloc(self, name, shape, dtype, align=64):
        self.off = (self.off + align - 1) // align * align
        per_part = int(np.prod(shape[1:])) * mybir.dt.size(dtype)
        h = self.nc.alloc_sbuf_tensor_at(f"{name}_{self.n}", list(shape), dtype,
                                         offset=self.off)
        self.n += 1
        self.off += per_part
        if self.off > self.limit:
            raise RuntimeError(f"SBUF over budget: {self.off} at {name}")
        return h

    def mark(self):
        self.marks.append(self.off)

    def release(self):
        self.off = self.marks.pop()


def bcast_mid(h, part_stride, nrep, width, offset=0, part_count=128):
    return bass.AP(h, offset, [[part_stride, part_count], [0, nrep], [1, width]])


def rep_cols(h, part_stride, ncols, col_off, rep, part_count=128):
    return bass.AP(h, col_off, [[part_stride, part_count], [1, ncols], [0, rep]])


class Ker:

    def __init__(self, nc):
        self.nc = nc
        self.P = Prog(nc)
        self.ar = Arena(nc)
        self.pb = [nc.alloc_psum_tensor(f"pb{i}", [128, 512], F32) for i in range(6)]
        self.pbf = [nc.alloc_psum_tensor(f"pbf{i}", [128, 1024], BF16) for i in range(2)]
        self.io = {}
        self.C = {}

    def mm_group(self, bank, c0, c1, terms, reads):
        key = ("pb", bank)
        pb = self.pb[bank]
        n = len(terms)
        for i, (l, r) in enumerate(terms):
            self.P.emit("t", (lambda t, o=pb[:, c0:c1], L=l, R=r, st=(i == 0),
                              sp=(i == n - 1): t.matmul(o, L, R, start=st, stop=sp)),
                        reads=list(reads), writes=(key,))

    def mm_group_ap(self, out_ap, bank_key, terms, reads):
        n = len(terms)
        for i, (l, r) in enumerate(terms):
            self.P.emit("t", (lambda t, o=out_ap, L=l, R=r, st=(i == 0),
                              sp=(i == n - 1): t.matmul(o, L, R, start=st, stop=sp)),
                        reads=list(reads), writes=(bank_key,))

    def transpose(self, bfbank, col, src_ap, reads):
        key = ("pbf", bfbank)
        pb = self.pbf[bfbank]
        self.P.emit("t", (lambda t, o=pb[:, col:col + 128], i_=src_ap,
                          idm=self.C["id_bf"][:, :]: t.transpose(o, i_, idm)),
                    reads=list(reads) + ["id_bf"], writes=(key,))

    def ev(self, eng, fn, reads=(), writes=()):
        self.P.emit(eng, fn, reads=reads, writes=writes)

    def dma(self, q, out_ap, in_ap, reads=(), writes=()):
        self.P.emit(q, (lambda e, a=out_ap, b=in_ap: e.dma_start(a, b)),
                    reads=reads, writes=writes, dma=True)

    def dbg(self, name, src_ap, shape, dtype, reads):
        if not DEBUG:
            return
        h = self.nc.dram_tensor(f"dbg_{name}", list(shape), dtype,
                                kind="ExternalOutput").ap()
        self.dma("p", h, src_ap, reads=reads, writes=((f"dbg_{name}",),))

    def dma_t(self, q, out_ap, in_ap, reads=(), writes=()):
        self.P.emit(q, (lambda e, a=out_ap, b=in_ap: e.dma_start(a, b, transpose=True)),
                    reads=reads, writes=writes, dma=True)



def host_constants():
    c = {}
    jj = np.arange(1024, dtype=np.int64)
    ang = (np.outer(jj, jj) % 1024).astype(np.float64) * (2 * np.pi / 1024.0)
    c["c1"] = np.cos(ang).astype(ml_dtypes.bfloat16)
    c["c2"] = np.sin(ang).astype(ml_dtypes.bfloat16)
    c["c2n"] = (-np.sin(ang)).astype(ml_dtypes.bfloat16)
    s = np.arange(128)[:, None]
    t = np.arange(128)[None, :]
    c["ut01"] = (s <= t).astype(np.float32)
    c["cumm"] = (s <= t).astype(np.float32)
    c["maskd"] = np.where(s <= t, 0.0, -3.0e38).astype(np.float32)
    c["id_bf"] = np.eye(128, dtype=ml_dtypes.bfloat16)
    c["id_f32"] = np.eye(128, dtype=np.float32)
    half = 64
    inv_freq = 1.0 / (10000.0 ** (np.arange(half, dtype=np.float64) / half))
    freqs = np.arange(T, dtype=np.float64)[:, None] * inv_freq[None, :]
    c["ropecos"] = np.cos(freqs).astype(np.float32)
    c["ropesin"] = np.sin(freqs).astype(np.float32)
    return c


CONST_SPECS = {
    "c1": ([1024, 1024], BF16), "c2": ([1024, 1024], BF16), "c2n": ([1024, 1024], BF16),
    "ut01": ([128, 128], F32), "cumm": ([128, 128], F32), "maskd": ([128, 128], F32),
    "id_bf": ([128, 128], BF16), "id_f32": ([128, 128], F32),
    "ropecos": ([T, 64], F32), "ropesin": ([T, 64], F32),
}


def load_const(K, name):
    shape, dt = CONST_SPECS[name]
    h = K.ar.alloc(name, shape, dt)
    K.C[name] = h
    K.dma("y", h[:, :], K.io[name], writes=(name,))
    return h



def phase0_l0(K):
    nc, P, ar = K.nc, K.P, K.ar
    W = {}
    for nm in ("id_bf", "ut01"):
        if nm not in K.C:
            load_const(K, nm)
    if "ones_bf" not in K.C:
        ones_bf = ar.alloc("ones_bf", [1, 512], BF16)
        K.C["ones_bf"] = ones_bf
        K.ev("p", lambda g: g.memset(ones_bf[:, :], 1.0), writes=("ones_bf",))

    for nm in ("wq0", "wk0", "wv0", "wo0"):
        cache = nc.dram_tensor(f"{nm}_c", [1024, 1024], BF16, kind="Internal").ap()
        K.dma("p", cache, K.io[nm], writes=(f"{nm}_c",))
        wt = ar.alloc(nm + "T", [128, 8, 1024], BF16)
        W[nm + "T"] = wt
        for dc in range(8):
            K.dma_t("y", wt[:, dc, :], cache[0:1024, 128 * dc:128 * dc + 128],
                    reads=(f"{nm}_c",), writes=((nm + "T", dc),))
    fbias = ar.alloc("fbias", [1, 1024], BF16)
    W["fbias"] = fbias
    K.dma("p", fbias[:, :], K.io["fbias"].rearrange("(a d) -> a d", a=1), writes=("fbias",))

    mc = ar.alloc("mc", [128, 8, 1024], BF16)
    W["mc"] = mc

    ar.mark()
    cc = {}
    for nm in ("c1", "c2"):
        h = ar.alloc(nm, [128, 8, 1024], BF16)
        cc[nm] = h
        K.dma("y", h[:, :, :], K.io[nm].rearrange("(c p) o -> p c o", p=128),
              writes=(nm,))
    for nm, key in (("wfr", "wr"), ("wfi", "wi")):
        h = ar.alloc(key, [128, 8, 1024], BF16)
        cc[key] = h
        K.dma("p", h[:, :, :], K.io[nm].rearrange("(c p) o -> p c o", p=128),
              writes=(key,))
    prt = ar.alloc("prt", [128, 8, 1024], BF16)
    pit = ar.alloc("pit", [128, 8, 1024], BF16)

    def build_mat(dst, dst_key, lhs_pairs, scale):
        for ic in range(8):
            for jg in range(2):
                bank = jg
                terms = []
                for kc in range(8):
                    for (lh, lkey, rh, rkey) in lhs_pairs:
                        terms.append((lh[:, kc, 128 * ic:128 * ic + 128],
                                      rh[:, kc, 512 * jg:512 * jg + 512]))
                reads = [k for (_, k, _, _) in lhs_pairs] + [k for (_, _, _, k) in lhs_pairs]
                K.mm_group(bank, 0, 512, terms, reads)
                K.ev("s", (lambda s, o=dst[:, ic, 512 * jg:512 * jg + 512],
                           i_=K.pb[bank][:, 0:512], sc=scale:
                           s.activation(o, i_, AF.Copy, scale=np.float32(sc))),
                     reads=[("pb", bank)], writes=[(dst_key, ic, jg)])

    build_mat(prt, "prt", [(cc["wr"], "wr", cc["c1"], "c1"),
                           (cc["wi"], "wi", cc["c2"], "c2")], 1.0)
    K.ev("v", (lambda v, o=cc["wi"][:, :, :], i_=cc["wi"][:, :, :]:
               v.tensor_scalar_mul(o, i_, -1.0)),
         reads=["wi"], writes=["wi"])
    build_mat(pit, "pit", [(cc["wi"], "wi", cc["c1"], "c1"),
                           (cc["wr"], "wr", cc["c2"], "c2")], 1.0)
    for jc in range(8):
        for og in range(2):
            bank = og
            terms = []
            for kc in range(8):
                terms.append((prt[:, kc, 128 * jc:128 * jc + 128],
                              cc["c1"][:, kc, 512 * og:512 * og + 512]))
                terms.append((pit[:, kc, 128 * jc:128 * jc + 128],
                              cc["c2"][:, kc, 512 * og:512 * og + 512]))
            reads = [("prt", kc, g) for kc in range(8) for g in range(2)] + \
                    [("pit", kc, g) for kc in range(8) for g in range(2)] + ["c1", "c2"]
            K.mm_group(bank, 0, 512, terms, reads)
            K.ev("s", (lambda s, o=mc[:, jc, 512 * og:512 * og + 512],
                       i_=K.pb[bank][:, 0:512]:
                       s.activation(o, i_, AF.Copy, scale=np.float32(1.0 / 1024.0))),
                 reads=[("pb", bank)], writes=[("mc", jc, og)])
    P.barrier()
    ar.release()
    return W



def ln_ops(K, src_ap, src_key, sq_scr, sq_key, cols, cols_key, eps):
    K.ev("s", (lambda s, o=sq_scr[:, :], i_=src_ap, a=cols[:, 1:2]:
               s.activation(o, i_, AF.Square, accum_out=a)),
         reads=[src_key], writes=[sq_key, (cols_key, "ssq")])
    K.ev("s", (lambda s, o=sq_scr[:, :], i_=src_ap, a=cols[:, 0:1]:
               s.activation(o, i_, AF.Identity, accum_out=a)),
         reads=[src_key], writes=[sq_key, (cols_key, "sx")])
    K.ev("v", (lambda v, o=cols[:, 2:3], i_=cols[:, 0:1]:
               v.tensor_scalar_mul(o, i_, 1.0 / D)),
         reads=[(cols_key, "sx")], writes=[(cols_key, "mean")])
    K.ev("v", (lambda v, o=cols[:, 4:5], i_=cols[:, 2:3], m=cols[:, 2:3]:
               v.tensor_scalar(o, i_, m, None, ALU.mult)),
         reads=[(cols_key, "mean")], writes=[(cols_key, "msq")])
    K.ev("v", (lambda v, o=cols[:, 5:6], i_=cols[:, 1:2], m=cols[:, 4:5]:
               v.tensor_scalar(o, i_, 1.0 / D, m, ALU.mult, ALU.subtract)),
         reads=[(cols_key, "ssq"), (cols_key, "msq")], writes=[(cols_key, "var")])
    K.ev("s", (lambda s, o=cols[:, 4:5], i_=cols[:, 5:6], b=eps:
               s.activation(o, i_, AF.Ln, bias=np.float32(b))),
         reads=[(cols_key, "var")], writes=[(cols_key, "lnv")])
    K.ev("s", (lambda s, o=cols[:, 3:4], i_=cols[:, 4:5]:
               s.activation(o, i_, AF.Exp, scale=np.float32(-0.5))),
         reads=[(cols_key, "lnv")], writes=[(cols_key, "rstd")])


P1LIM = 10 ** 9
P1ATT = 99
P1HEADS = 99
DEBUG = False
DBG_CI = 0


def phase1(K, W, moe_casts=None, nch=NCH, cast_floor=0):
    nc, P, ar = K.nc, K.P, K.ar
    sec = [0]

    def lim():
        sec[0] += 1
        return sec[0] <= P1LIM
    ar.mark()
    xin = [ar.alloc(f"xin{i}", [128, 1024], F32) for i in range(2)]
    lnsc = ar.alloc("lnsc", [128, 8], F32)
    sqs = ar.alloc("sqs", [128, 1024], F32)
    xnbf = ar.alloc("xnbf", [128, 1024], BF16)
    xnfm = ar.alloc("xnfm", [128, 8, 128], BF16)
    qpad = ar.alloc("qpad", [128, 16, 128], BF16)
    ktm = ar.alloc("ktm", [128, 1024], BF16)
    kpad = ar.alloc("kpad", [128, 16, 128], BF16)
    vtm = ar.alloc("vtm", [128, 16, 65], BF16)
    st0 = ar.alloc("st0", [128, 8, 65], F32)
    stbf = ar.alloc("stbf", [128, 8, 65], BF16)
    abuf = ar.alloc("abuf", [128, 8, 128], BF16)
    den = ar.alloc("den", [128, 16], F32)
    rden = ar.alloc("rden", [128, 16], F32)
    yall = ar.alloc("yall", [128, 1024], BF16)
    yfm = ar.alloc("yfm", [128, 8, 128], BF16)
    te1 = ar.alloc("te1", [128, 1024], F32)
    te2 = ar.alloc("te2", [128, 1024], F32)
    te3 = ar.alloc("te3", [128, 1024], F32)
    gbuf = ar.alloc("gbuf", [128, 1024], F32)
    x1t = ar.alloc("x1t", [128, 1024], F32)
    xn1f = ar.alloc("xn1f", [128, 1024], F32)
    xn1b = ar.alloc("xn1b", [128, 1024], BF16)

    K.ev("v", (lambda v, o=bass.AP(vtm, 64, [[16 * 65, 128], [65, 16], [1, 1]]):
               v.memset(o, 1.0)), writes=[("vtm", "ones")])
    K.ev("v", lambda v, o=st0[:, :, :]: v.memset(o, 0.0), writes=["st0"])
    K.ev("v", lambda v, o=qpad[:, :, :]: v.memset(o, 0.0), writes=["qpad"])
    K.ev("v", lambda v, o=kpad[:, :, :]: v.memset(o, 0.0), writes=["kpad"])


    for ci in range(nch):
        par = ci % 2
        xk = f"xin{par}"
        rows = slice(CH * ci, CH * ci + CH)
        K.dma("y", xin[par][:, :], K.io["x"][rows, :], writes=[xk])
        if lim():
            ln_ops(K, xin[par][:, :], xk, sqs, "sqs", lnsc, "lnsc", 1e-5)
            K.ev("v", (lambda v, o=xnbf[:, :], i_=xin[par][:, :], m=lnsc[:, 2:3],
                       r=lnsc[:, 3:4]:
                       v.tensor_scalar(o, i_, m, r, ALU.subtract, ALU.mult)),
                 reads=[xk, ("lnsc", "mean"), ("lnsc", "rstd")], writes=["xnbf"])
        if lim():
            for half in range(2):
                bb = half
                for q in range(4):
                    bc = 4 * half + q
                    K.transpose(bb, 128 * q, xnbf[:, 128 * bc:128 * bc + 128], ["xnbf"])
                K.ev("s", (lambda s, o=xnfm[:, 4 * half:4 * half + 4, :],
                           i_=K.pbf[bb][:, 0:512]: s.copy(o, i_)),
                     reads=[("pbf", bb)], writes=[("xnfm", half)])
            xnfm_keys = [("xnfm", 0), ("xnfm", 1)]
        if lim():
            for ec in range(8):
                bank = ec // 4
                c0 = 128 * (ec % 4)
                terms = [(W["wq0T"][:, kc, 128 * ec:128 * ec + 128], xnfm[:, kc, :])
                         for kc in range(8)]
                K.mm_group(bank, c0, c0 + 128, terms,
                           [("wq0T", kc) for kc in range(8)] + xnfm_keys)
        if lim():
            for bank in range(2):
                K.ev("v", (lambda v, o=te1[:, 512 * bank:512 * bank + 512],
                           i_=K.pb[bank][:, 0:512]:
                           v.tensor_scalar_min(o, i_, 0.0)),
                     reads=[("pb", bank)], writes=[("te1", bank)])
                K.ev("v", (lambda v, o=te2[:, 512 * bank:512 * bank + 512],
                           i_=K.pb[bank][:, 0:512]:
                           v.tensor_scalar_max(o, i_, 0.0)),
                     reads=[("pb", bank)], writes=[("te2", bank)])
            K.ev("s", (lambda s, o=te1[:, :], i_=te1[:, :]:
                       s.activation(o, i_, AF.Exp)),
                 reads=[("te1", 0), ("te1", 1)], writes=[("te1", 0), ("te1", 1)])
            K.ev("v", (lambda v, o=qpad[0:64, 0:16:2, :],
                       a=te1[0:64, :].rearrange("p (c w) -> p c w", w=128),
                       b=te2[0:64, :].rearrange("p (c w) -> p c w", w=128):
                       v.tensor_add(o, a, b)),
                 reads=[("te1", 0), ("te1", 1), ("te2", 0), ("te2", 1)],
                 writes=["qpad"])
            K.ev("v", (lambda v, o=qpad[64:128, 1:16:2, :],
                       a=te1[64:128, :].rearrange("p (c w) -> p c w", w=128),
                       b=te2[64:128, :].rearrange("p (c w) -> p c w", w=128):
                       v.tensor_add(o, a, b)),
                 reads=[("te1", 0), ("te1", 1), ("te2", 0), ("te2", 1)],
                 writes=["qpad"])
        if lim():
            for bank, eg in ((2, 0), (3, 1)):
                terms = [(xnfm[:, kc, :], W["wk0T"][:, kc, 512 * eg:512 * eg + 512])
                         for kc in range(8)]
                K.mm_group(bank, 0, 512, terms,
                           [("wk0T", kc) for kc in range(8)] + xnfm_keys)
            for bank in (2, 3):
                b = bank - 2
                K.ev("v", (lambda v, o=te1[:, 512 * b:512 * b + 512],
                           i_=K.pb[bank][:, 0:512]: v.tensor_scalar_min(o, i_, 0.0)),
                     reads=[("pb", bank)], writes=[("te1", b)])
                K.ev("v", (lambda v, o=te2[:, 512 * b:512 * b + 512],
                           i_=K.pb[bank][:, 0:512]: v.tensor_scalar_max(o, i_, 0.0)),
                     reads=[("pb", bank)], writes=[("te2", b)])
            K.ev("s", (lambda s, o=te1[:, :], i_=te1[:, :]: s.activation(o, i_, AF.Exp)),
                 reads=[("te1", 0), ("te1", 1)], writes=[("te1", 0), ("te1", 1)])
            K.ev("v", (lambda v, o=ktm[:, :], a=te1[:, :], b=te2[:, :]:
                       v.tensor_add(o, a, b)),
                 reads=[("te1", 0), ("te1", 1), ("te2", 0), ("te2", 1)], writes=["ktm"])
        if lim():
            for half in range(2):
                bb = half
                for q in range(4):
                    bc = 4 * half + q
                    K.transpose(bb, 128 * q, ktm[:, 128 * bc:128 * bc + 128], ["ktm"])
                K.ev("s", (lambda s, o=kpad[0:64, 8 * half:8 * half + 8:2, :],
                           i_=K.pbf[bb][0:64, 0:512].rearrange("p (c w) -> p c w", w=128):
                           s.copy(o, i_)),
                     reads=[("pbf", bb)], writes=["kpad"])
                K.ev("s", (lambda s, o=kpad[64:128, 8 * half + 1:8 * half + 8:2, :],
                           i_=K.pbf[bb][64:128, 0:512].rearrange("p (c w) -> p c w", w=128):
                           s.copy(o, i_)),
                     reads=[("pbf", bb)], writes=["kpad"])
            kfm_keys = ["kpad"]
        if lim():
            for bank, eg in ((4, 0), (5, 1)):
                terms = [(xnfm[:, kc, :], W["wv0T"][:, kc, 512 * eg:512 * eg + 512])
                         for kc in range(8)]
                K.mm_group(bank, 0, 512, terms,
                           [("wv0T", kc) for kc in range(8)] + xnfm_keys)
                K.ev("v", (lambda v, o=vtm[:, 8 * eg:8 * eg + 8, 0:64],
                           i_=K.pb[bank][:, 0:512].rearrange("p (h w) -> p h w", h=8):
                           v.tensor_copy(o, i_)),
                     reads=[("pb", bank)], writes=[("vtm", eg)])
            vtm_keys = [("vtm", 0), ("vtm", 1), ("vtm", "ones")]
        if lim():
            K.ev("s", (lambda s, o=stbf[:, :, :], i_=st0[:, :, :]: s.copy(o, i_)),
                 reads=["st0"], writes=["stbf"])
        if lim():
            for w in range(2):
                for hh in range(min(8, P1HEADS)):
                    h = 8 * w + hh
                    bank = hh // 4
                    c0 = 128 * (hh % 4)
                    lhsT = kpad[:, h, :]
                    rhs = qpad[:, h, :]
                    K.mm_group(bank, c0, c0 + 128, [(lhsT, rhs)], kfm_keys + ["qpad"])
                for bank in range(2) if P1ATT >= 2 else []:
                    K.ev("v", (lambda v, o=abuf[:, 4 * bank:4 * bank + 4, :],
                               i0=K.pb[bank][:, 0:512].rearrange("p (h w) -> p h w", h=4),
                               i1=bcast_mid(K.C["ut01"], 128, 4, 128):
                               v.tensor_tensor(o, i0, i1, ALU.mult)),
                         reads=[("pb", bank), "ut01"], writes=[("abuf", bank)])
                for hh in range(8) if P1ATT >= 3 else []:
                    h = 8 * w + hh
                    bank = 2 + hh // 4
                    c0 = 128 * (hh % 4)
                    terms = [(abuf[:, hh, :], vtm[:, h, :]),
                             (qpad[:, h, :], stbf[:, h // 2, :])]
                    K.mm_group(bank, c0, c0 + 65, terms,
                               [("abuf", hh // 4)] + vtm_keys + ["qpad", "stbf"])
                for bank in ((2, 3) if P1ATT >= 4 else ()):
                    b = bank - 2
                    K.ev("v", (lambda v, o=den[:, 8 * w + 4 * b:8 * w + 4 * b + 4],
                               i_=bass.AP(K.pb[bank], 64, [[512, 128], [128, 4]]):
                               v.tensor_copy(o, i_)),
                         reads=[("pb", bank)], writes=[("den", w, b)])
                if P1ATT >= 4:
                    K.ev("v", (lambda v, o=den[:, 8 * w:8 * w + 8], i_=den[:, 8 * w:8 * w + 8]:
                               v.tensor_scalar_max(o, i_, 1e-6)),
                         reads=[("den", w, 0), ("den", w, 1)], writes=[("den", w)])
                    K.ev("v", (lambda v, o=rden[:, 8 * w:8 * w + 8], i_=den[:, 8 * w:8 * w + 8]:
                               v.reciprocal(o, i_)),
                         reads=[("den", w)], writes=[("rden", w)])
                for bank in ((2, 3) if P1ATT >= 5 else ()):
                    b = bank - 2
                    K.ev("v", (lambda v,
                               o=bass.AP(yall, 512 * w + 256 * b, [[1024, 128], [64, 4], [1, 64]]),
                               i0=bass.AP(K.pb[bank], 0, [[512, 128], [128, 4], [1, 64]]),
                               i1=rep_cols(rden, 16, 4, 8 * w + 4 * b, 64):
                               v.tensor_tensor(o, i0, i1, ALU.mult)),
                         reads=[("pb", bank), ("rden", w)], writes=[("yall", w, b)])
                for hh in range(8) if P1ATT >= 6 else []:
                    h = 8 * w + hh
                    bank = 4 + hh // 4
                    j = hh % 4
                    c0 = 128 * j
                    terms = [(ktm[:, 128 * (h // 2):128 * (h // 2) + 128],
                              vtm[:, h, :])]
                    K.mm_group(bank, c0, c0 + 65, terms, ["ktm"] + vtm_keys)
                if P1ATT >= 7:
                    for hh in range(8):
                        h = 8 * w + hh
                        bank = 4 + hh // 4
                        j = hh % 4
                        hf = 64 * (h % 2)
                        K.ev("v", (lambda v,
                                   o=st0[hf:hf + 64, h // 2, :],
                                   i0=K.pb[bank][hf:hf + 64, 128 * j:128 * j + 65],
                                   i1=st0[hf:hf + 64, h // 2, :]:
                                   v.tensor_tensor(o, i0, i1, ALU.add)),
                             reads=[("pb", bank), "st0"], writes=["st0"])

            yall_keys = [("yall", w, b) for w in range(2) for b in range(2)]
        if lim():
            for half in range(2):
                bb = half
                for q in range(4):
                    bc = 4 * half + q
                    K.transpose(bb, 128 * q, yall[:, 128 * bc:128 * bc + 128], yall_keys)
                K.ev("s", (lambda s, o=yfm[:, 4 * half:4 * half + 4, :],
                           i_=K.pbf[bb][:, 0:512]: s.copy(o, i_)),
                     reads=[("pbf", bb)], writes=[("yfm", half)])
            yfm_keys = [("yfm", 0), ("yfm", 1)]
        if lim():
            for bank, eg in ((2, 0), (3, 1)):
                terms = [(yfm[:, kc, :], W["wo0T"][:, kc, 512 * eg:512 * eg + 512])
                         for kc in range(8)]
                K.mm_group(bank, 0, 512, terms,
                           [("wo0T", kc) for kc in range(8)] + yfm_keys)
        if lim():
            for bank, og in ((0, 0), (1, 1)):
                terms = [(xnfm[:, kc, :], W["mc"][:, kc, 512 * og:512 * og + 512])
                         for kc in range(8)]
                terms.append((K.C["ones_bf"][0:1, 0:128],
                              W["fbias"][0:1, 512 * og:512 * og + 512]))
                K.mm_group(bank, 0, 512, terms,
                           [("mc", kc, og) for kc in range(8)] + xnfm_keys +
                           ["fbias", "ones_bf"])
        if DEBUG and ci == DBG_CI:
            for bank in range(2):
                K.ev("s", (lambda s, o=te3[:, 512 * bank:512 * bank + 512],
                           i_=K.pb[bank][:, 0:512]: s.copy(o, i_)),
                     reads=[("pb", bank)], writes=["te3"])
            K.dbg("ffnz", te3[:, :], [128, 1024], F32, ["te3"])
            for bank in (2, 3):
                K.ev("s", (lambda s, o=te2[:, 512 * (bank - 2):512 * (bank - 2) + 512],
                           i_=K.pb[bank][:, 0:512]: s.copy(o, i_)),
                     reads=[("pb", bank)], writes=[("te2", bank - 2)])
            K.dbg("stm", te2[:, :], [128, 1024], F32, [("te2", 0), ("te2", 1)])
            K.dbg("yfm", yfm[:, :, :], [128, 8, 128], BF16, [("yfm", 0), ("yfm", 1)])
        if lim():
            for bank in range(2):
                K.ev("v", (lambda v, o=te1[:, 512 * bank:512 * bank + 512],
                           i_=K.pb[bank][:, 0:512]: v.tensor_scalar_mul(o, i_, -1.0)),
                     reads=[("pb", bank)], writes=[("te1", bank)])
                K.ev("v", (lambda v, o=te2[:, 512 * bank:512 * bank + 512],
                           i0=K.pb[bank][:, 0:512], i1=te1[:, 512 * bank:512 * bank + 512]:
                           v.tensor_tensor(o, i0, i1, ALU.min)),
                     reads=[("pb", bank), ("te1", bank)], writes=[("te2", bank)])
            K.ev("s", (lambda s, o=te2[:, :], i_=te2[:, :]: s.activation(o, i_, AF.Exp)),
                 reads=[("te2", 0), ("te2", 1)], writes=[("te2", 0), ("te2", 1)])
            K.ev("s", (lambda s, o=te3[:, :], i_=te2[:, :]:
                       s.activation(o, i_, AF.Ln, bias=np.float32(1.0))),
                 reads=[("te2", 0), ("te2", 1)], writes=["te3"])
            K.ev("v", (lambda v, o=te1[:, :], i_=te1[:, :]: v.tensor_scalar_max(o, i_, 0.0)),
                 reads=[("te1", 0), ("te1", 1)], writes=[("te1", 0), ("te1", 1)])
            K.ev("v", (lambda v, o=te3[:, :], a=te1[:, :], b=te3[:, :]:
                       v.tensor_add(o, a, b)),
                 reads=[("te1", 0), ("te1", 1), "te3"], writes=["te3"])
            K.ev("s", (lambda s, o=te3[:, :], i_=te3[:, :]:
                       s.activation(o, i_, AF.Exp, scale=np.float32(-1.0))),
                 reads=["te3"], writes=["te3"])
            for bank in range(2):
                K.ev("v", (lambda v, o=gbuf[:, 512 * bank:512 * bank + 512],
                           i0=K.pb[bank][:, 0:512], i1=te3[:, 512 * bank:512 * bank + 512]:
                           v.tensor_tensor(o, i0, i1, ALU.mult)),
                     reads=[("pb", bank), "te3"], writes=[("gbuf", bank)])
        if lim():
            for bank in (2, 3):
                b = bank - 2
                K.ev("v", (lambda v, o=te1[:, 512 * b:512 * b + 512],
                           i0=K.pb[bank][:, 0:512], i1=gbuf[:, 512 * b:512 * b + 512]:
                           v.tensor_tensor(o, i0, i1, ALU.mult)),
                     reads=[("pb", bank), ("gbuf", b)], writes=[("te1", b)])
            K.ev("v", (lambda v, o=x1t[:, :], a=xin[par][:, :], b=te1[:, :]:
                       v.tensor_add(o, a, b)),
                 reads=[xk, ("te1", 0), ("te1", 1)], writes=["x1t"])
        if lim():
            ln_ops(K, x1t[:, :], "x1t", sqs, "sqs", lnsc, "lnsc", 1e-5)
            K.ev("v", (lambda v, o=xn1f[:, :], i_=x1t[:, :], m=lnsc[:, 2:3],
                       r=lnsc[:, 3:4]:
                       v.tensor_scalar(o, i_, m, r, ALU.subtract, ALU.mult)),
                 reads=["x1t", ("lnsc", "mean"), ("lnsc", "rstd")], writes=["xn1f"])
            K.ev("v", (lambda v, o=xn1b[:, :], i_=xn1f[:, :]: v.tensor_copy(o, i_)),
                 reads=["xn1f"], writes=["xn1b"])
        if ci == DBG_CI:
            K.dbg("xnbf", xnbf[:, :], [128, 1024], BF16, ["xnbf"])
            K.dbg("qpad", qpad[:, :, :], [128, 16, 128], BF16, ["qpad"])
            K.dbg("kpad", kpad[:, :, :], [128, 16, 128], BF16, ["kpad"])
            K.dbg("ktm", ktm[:, :], [128, 1024], BF16, ["ktm"])
            K.dbg("vtm", vtm[:, :, :], [128, 16, 65], BF16,
                  [("vtm", 0), ("vtm", 1), ("vtm", "ones")])
            K.dbg("den", den[:, :], [128, 16], F32, [("den", 0), ("den", 1)])
            K.dbg("yall", yall[:, :], [128, 1024], BF16,
                  [("yall", w2, b2) for w2 in range(2) for b2 in range(2)])
            K.dbg("gbuf", gbuf[:, :], [128, 1024], F32, [("gbuf", 0), ("gbuf", 1)])
            K.dbg("st0", st0[:, :, :], [128, 8, 65], F32, ["st0"])
        K.dma("p", K.io["x1"][rows, :], x1t[:, :], reads=["x1t"], writes=[("x1_h", ci)])
        K.dma("p", K.io["xn1f"][rows, :], xn1f[:, :], reads=["xn1f"],
              writes=[("xn1f_h", ci)])
        K.dma("p", K.io["xn1b"][rows, :], xn1b[:, :], reads=["xn1b"],
              writes=[("xn1b_h", ci)])
        if moe_casts:
            while moe_casts and len(moe_casts) > max(cast_floor,
                                                     (nch - 1 - ci) * 2):
                moe_casts.pop(0)()

    P.barrier()
    ar.release()
    return moe_casts


def moe_cast_list(K):
    nc = K.nc
    thunks = []
    K.moe_cache = {}
    w1c = nc.dram_tensor("w1c", [9, 2048, 1024], BF16, kind="Internal").ap()
    w3c = nc.dram_tensor("w3c", [9, 2048, 1024], BF16, kind="Internal").ap()
    w2c = nc.dram_tensor("w2c", [9, 1024, 1024], BF16, kind="Internal").ap()
    K.moe_cache.update(w1c=w1c, w3c=w3c, w2c=w2c)

    def mk(dst_ap, src_ap, key):
        def th():
            K.dma("p", dst_ap, src_ap, writes=[key])
        return th

    for e in range(8):
        thunks.append(mk(w1c[e, 0:1024, :], K.io["w1"][e, 0:2048:2, :], ("w1c", e, 0)))
        thunks.append(mk(w1c[e, 1024:2048, :], K.io["w1"][e, 1:2048:2, :], ("w1c", e, 1)))
        thunks.append(mk(w3c[e, 0:1024, :], K.io["w3"][e, 0:2048:2, :], ("w3c", e, 0)))
        thunks.append(mk(w3c[e, 1024:2048, :], K.io["w3"][e, 1:2048:2, :], ("w3c", e, 1)))
        thunks.append(mk(w2c[e, :, :], K.io["w2"][e, :, :], ("w2c", e)))
    thunks.append(mk(w1c[8, 0:1024, :], K.io["sw1"][0:2048:2, :], ("w1c", 8, 0)))
    thunks.append(mk(w1c[8, 1024:2048, :], K.io["sw1"][1:2048:2, :], ("w1c", 8, 1)))
    thunks.append(mk(w3c[8, 0:1024, :], K.io["sw3"][0:2048:2, :], ("w3c", 8, 0)))
    thunks.append(mk(w3c[8, 1024:2048, :], K.io["sw3"][1:2048:2, :], ("w3c", 8, 1)))
    thunks.append(mk(w2c[8, :, :], K.io["sw2"][:, :], ("w2c", 8)))
    return thunks





def phase2_prep(K):
    nc, P, ar = K.nc, K.P, K.ar
    W = {}
    for nm in ("cumm", "maskd", "id_f32", "id_bf"):
        if nm not in K.C:
            load_const(K, nm)
    for nm in ("ropecos", "ropesin"):
        if nm not in K.C:
            h = ar.alloc(nm, [128, 32, 64], F32)
            K.C[nm] = h
            K.dma("y", h[:, :, :], K.io[nm].rearrange("(c p) i -> p c i", p=128),
                  writes=(nm,))
    if "ones_f32" not in K.C:
        ones_f32 = ar.alloc("ones_f32", [128, 128], F32)
        K.C["ones_f32"] = ones_f32
        K.ev("p", lambda g: g.memset(ones_f32[:, :], 1.0), writes=("ones_f32",))
    ones_f32 = K.C["ones_f32"]
    if "ones_bf" not in K.C:
        ones_bf = ar.alloc("ones_bf", [1, 512], BF16)
        K.C["ones_bf"] = ones_bf
        K.ev("p", lambda g: g.memset(ones_bf[:, :], 1.0), writes=("ones_bf",))

    nw = ar.alloc("nw", [128, 8], F32)
    for dc in range(8):
        K.dma("p", nw[:, dc:dc + 1],
              K.io["norm_w"][128 * dc:128 * dc + 128].rearrange("(p a) -> p a", a=1),
              writes=(("nw", dc),))
    nw_keys = [("nw", dc) for dc in range(8)]

    for nm, scale_nw in (("wq1", True), ("wk1", True), ("wv1", True),
                         ("wout1", False)):
        cache = nc.dram_tensor(f"{nm}_c", [1024, 1024], BF16, kind="Internal").ap()
        K.dma("p", cache, K.io[nm], writes=(f"{nm}_c",))
        wt = ar.alloc(nm + "T", [128, 8, 1024], BF16)
        W[nm + "T"] = wt
        for dc in range(8):
            K.dma_t("y", wt[:, dc, :], cache[0:1024, 128 * dc:128 * dc + 128],
                    reads=(f"{nm}_c",), writes=((nm + "T", dc),))
            if scale_nw:
                K.ev("v", (lambda v, o=wt[:, dc, :], i_=wt[:, dc, :],
                           s1=nw[:, dc:dc + 1]:
                           v.tensor_scalar_mul(o, i_, s1)),
                     reads=[(nm + "T", dc), ("nw", dc)], writes=[(nm + "T", dc)])

    gcache = nc.dram_tensor("g_c", [32, 1024], BF16, kind="Internal").ap()
    for j, nm in enumerate(("wbeta", "wig", "wog", "gate_w")):
        K.dma("p", gcache[8 * j:8 * j + 8, :], K.io[nm], writes=(("g_c", j),))
    gatesT = ar.alloc("gatesT", [128, 8, 32], BF16)
    W["gatesT"] = gatesT
    for dc in range(8):
        K.dma_t("y", gatesT[:, dc, :], gcache[0:32, 128 * dc:128 * dc + 128],
                reads=[("g_c", j) for j in range(4)], writes=(("gatesT", dc),))
        K.ev("v", (lambda v, o=gatesT[:, dc, 0:24], i_=gatesT[:, dc, 0:24],
                   s1=nw[:, dc:dc + 1]: v.tensor_scalar_mul(o, i_, s1)),
             reads=[("gatesT", dc), ("nw", dc)], writes=[("gatesT", dc)])

    gbias = ar.alloc("gbias", [1, 24], BF16)
    W["gbias"] = gbias
    for j, nm in enumerate(("bbeta", "big", "bog")):
        K.dma("p", gbias[0:1, 8 * j:8 * j + 8],
              K.io[nm].rearrange("(a e) -> a e", a=1), writes=(("gbias", j),))
    bout = ar.alloc("bout", [1, 1024], BF16)
    W["bout"] = bout
    K.dma("p", bout[:, :], K.io["bout"].rearrange("(a d) -> a d", a=1),
          writes=("bout",))
    gbrow = ar.alloc("gbrow", [1, 8], F32)
    K.dma("y", gbrow[:, :], K.io["gate_b"].rearrange("(a e) -> a e", a=1),
          writes=("gbrow",))
    gb_bc = ar.alloc("gb_bc", [128, 8], F32)
    W["gb_bc"] = gb_bc
    K.mm_group_ap(K.pb[2][:, 0:8], ("pb", 2),
                  [(ones_f32[0:1, :], gbrow[0:1, :])], ["ones_f32", "gbrow"])
    K.ev("v", (lambda v, o=gb_bc[:, :], i_=K.pb[2][:, 0:8]: v.tensor_copy(o, i_)),
         reads=[("pb", 2)], writes=["gb_bc"])

    M_sb = ar.alloc("M_sb", [128, 8, 128], F32)
    M_bf = ar.alloc("M_bf", [128, 8, 128], BF16)
    W["M_sb"], W["M_bf"] = M_sb, M_bf
    K.ev("v", lambda v, o=M_sb[:, :, :]: v.memset(o, 0.0), writes=["M_sb"])
    cumsc = ar.alloc("cumsc", [128, 128], F32)
    W["cumsc"] = cumsc
    K.ev("v", lambda v, o=cumsc[:, :]: v.memset(o, 0.0), writes=["cumsc"])
    return W


def phase2(K, W, moe_casts=None, nch=NCH):
    nc, P, ar = K.nc, K.P, K.ar
    ar.mark()
    xin = [ar.alloc(f"p2xin{i}", [128, 1024], F32) for i in range(2)]
    sc = ar.alloc("p2sc", [128, 8], F32)
    sqs = ar.alloc("p2sqs", [128, 1024], F32)
    r_bf = ar.alloc("r_bf", [128, 1024], BF16)
    rfm = ar.alloc("rfm", [128, 8, 128], BF16)
    q_ro = ar.alloc("q_ro", [128, 1024], BF16)
    k_ro = ar.alloc("k_ro", [128, 1024], BF16)
    v_bf = ar.alloc("v_bf", [128, 1024], BF16)
    khat = ar.alloc("khat", [128, 1024], BF16)
    qfm2 = ar.alloc("qfm2", [128, 8, 128], BF16)
    kfm2 = ar.alloc("kfm2", [128, 8, 128], BF16)
    zg = ar.alloc("zg", [128, 24], F32)
    nl = ar.alloc("nl", [128, 16], F32)
    og = ar.alloc("og", [128, 8], F32)
    cum = ar.alloc("cum", [128, 8], F32)
    colst = ar.alloc("colst", [128, 8], F32)
    ncC = ar.alloc("ncC", [128, 8], F32)
    wcol = ar.alloc("wcol", [128, 8], F32)
    decC = ar.alloc("decC", [128, 8], F32)
    expc = ar.alloc("expc", [128, 8], F32)
    cumT = ar.alloc("cumT", [128, 3, 128], F32)
    dsb = ar.alloc("dsb", [128, 8, 128], F32)
    abuf2 = ar.alloc("abuf2", [128, 8, 128], BF16)
    itmp = ar.alloc("itmp", [128, 1024], F32)
    y0 = ar.alloc("y0", [128, 1024], F32)
    y_bf = ar.alloc("y_bf", [128, 1024], BF16)
    yfm2 = ar.alloc("yfm2", [128, 8, 128], BF16)
    s1t = ar.alloc("s1t", [128, 1024], F32)
    te1 = ar.alloc("p2te1", [128, 1024], F32)
    te2 = ar.alloc("p2te2", [128, 1024], F32)
    lg = ar.alloc("lg", [128, 48], F32)
    cwt = ar.alloc("cwt", [128, 8], F32)
    idxc = ar.alloc("idxc", [128, 8], F32)
    idxp = ar.alloc("idxp", [128, 8], F32)
    K.ev("p", (lambda g, o=idxc[:, :]: g.iota(o, [[1, 8]], channel_multiplier=0,
                                              allow_small_or_imprecise_dtypes=True)),
         writes=["idxc"])
    K.ev("v", (lambda v, o=idxp[:, :], i_=idxc[:, :]:
               v.tensor_scalar_add(o, i_, 1.0e4)),
         reads=["idxc"], writes=["idxp"])
    CUMM = K.C["cumm"]
    MASKD = K.C["maskd"]
    ONESF = K.C["ones_f32"]

    base_cls_bank = {0: 2, 1: 4, 2: 5}

    for ci in range(nch):
        par = ci % 2
        xk = f"p2xin{par}"
        rows = slice(CH * ci, CH * ci + CH)
        K.dma("y", xin[par][:, :], K.io["xn1f"][rows, :], writes=[xk])
        K.ev("s", (lambda s, o=sqs[:, :], i_=xin[par][:, :], a=sc[:, 0:1]:
                   s.activation(o, i_, AF.Square, accum_out=a)),
             reads=[xk], writes=["p2sqs", ("sc", 0)])
        K.ev("s", (lambda s, o=sc[:, 1:2], i_=sc[:, 0:1]:
                   s.activation(o, i_, AF.Ln, scale=np.float32(1.0 / D),
                                bias=np.float32(1e-6))),
             reads=[("sc", 0)], writes=[("sc", 1)])
        K.ev("s", (lambda s, o=sc[:, 2:3], i_=sc[:, 1:2]:
                   s.activation(o, i_, AF.Exp, scale=np.float32(-0.5))),
             reads=[("sc", 1)], writes=[("sc", 2)])
        K.ev("v", (lambda v, o=sc[:, 3:4], i_=sc[:, 2:3]: v.reciprocal(o, i_)),
             reads=[("sc", 2)], writes=[("sc", 3)])
        K.ev("v", (lambda v, o=r_bf[:, :], i_=xin[par][:, :], s1=sc[:, 2:3]:
                   v.tensor_scalar_mul(o, i_, s1)),
             reads=[xk, ("sc", 2)], writes=["r_bf"])
        for half in range(2):
            bb = half
            for q in range(4):
                bc = 4 * half + q
                K.transpose(bb, 128 * q, r_bf[:, 128 * bc:128 * bc + 128], ["r_bf"])
            K.ev("s", (lambda s, o=rfm[:, 4 * half:4 * half + 4, :],
                       i_=K.pbf[bb][:, 0:512]: s.copy(o, i_)),
                 reads=[("pbf", bb)], writes=[("rfm", half)])
        rfm_keys = [("rfm", 0), ("rfm", 1)]

        g_reads = [("gatesT", kc) for kc in range(8)] + rfm_keys + \
                  ["ones_bf", ("gbias", 0), ("gbias", 1), ("gbias", 2)]
        for kc in range(8):
            K.P.emit("t", (lambda t, o=K.pb[3][:, 0:32], L=rfm[:, kc, :],
                           R=W["gatesT"][:, kc, :], st=(kc == 0):
                           t.matmul(o, L, R, start=st, stop=False)),
                     reads=g_reads, writes=[("pb", 3)])
        K.P.emit("t", (lambda t, o=K.pb[3][:, 0:24],
                       L=K.C["ones_bf"][0:1, 0:128], R=W["gbias"][0:1, :]:
                       t.matmul(o, L, R, start=False, stop=True)),
                 reads=g_reads, writes=[("pb", 3)])
        K.ev("v", (lambda v, o=zg[:, :], i_=K.pb[3][:, 0:24]: v.tensor_copy(o, i_)),
             reads=[("pb", 3)], writes=["zg"])
        K.ev("s", (lambda s, o=lg[:, 0:8], i_=K.pb[3][:, 24:32], sca=sc[:, 3:4]:
                   s.activation(o, i_, AF.Copy, scale=sca)),
             reads=[("pb", 3), ("sc", 3)], writes=[("lg", "logits")])

        K.ev("v", (lambda v, o=te1[:, 0:16], i_=zg[:, 0:16]:
                   v.tensor_scalar_mul(o, i_, -1.0)),
             reads=["zg"], writes=[("te1", "zn")])
        K.ev("v", (lambda v, o=te1[:, 16:32], i0=zg[:, 0:16], i1=te1[:, 0:16]:
                   v.tensor_tensor(o, i0, i1, ALU.min)),
             reads=["zg", ("te1", "zn")], writes=[("te1", "na")])
        K.ev("s", (lambda s, o=te1[:, 16:32], i_=te1[:, 16:32]:
                   s.activation(o, i_, AF.Exp)),
             reads=[("te1", "na")], writes=[("te1", "na")])
        K.ev("s", (lambda s, o=te1[:, 32:48], i_=te1[:, 16:32]:
                   s.activation(o, i_, AF.Ln, bias=np.float32(1.0))),
             reads=[("te1", "na")], writes=[("te1", "l")])
        K.ev("v", (lambda v, o=te1[:, 0:16], i_=te1[:, 0:16]:
                   v.tensor_scalar_max(o, i_, 0.0)),
             reads=[("te1", "zn")], writes=[("te1", "zn")])
        K.ev("v", (lambda v, o=nl[:, :], a=te1[:, 0:16], b=te1[:, 32:48]:
                   v.tensor_add(o, a, b)),
             reads=[("te1", "zn"), ("te1", "l")], writes=["nl"])
        K.ev("v", (lambda v, o=nl[:, :], i_=nl[:, :]:
                   v.tensor_scalar_min(o, i_, 80.0)),
             reads=["nl"], writes=["nl"])
        K.ev("s", (lambda s, o=og[:, :], i_=zg[:, 16:24]:
                   s.activation(o, i_, AF.Exp, scale=np.float32(-1.0))),
             reads=["zg"], writes=["og"])
        K.ev("v", (lambda v, o=og[:, :], i_=og[:, :]: v.tensor_scalar_add(o, i_, 1.0)),
             reads=["og"], writes=["og"])
        K.ev("v", (lambda v, o=og[:, :], i_=og[:, :]: v.reciprocal(o, i_)),
             reads=["og"], writes=["og"])

        K.mm_group_ap(K.pb[3][:, 64:72], ("pb", 3),
                      [(CUMM[:, :], nl[:, 0:8])], ["cumm", "nl"])
        K.ev("v", (lambda v, o=cum[:, :], i_=K.pb[3][:, 64:72]: v.tensor_copy(o, i_)),
             reads=[("pb", 3)], writes=["cum"])
        K.ev("v", (lambda v, o=colst[:, :], a=cum[:, :], b=nl[:, 8:16]:
                   v.tensor_sub(o, a, b)),
             reads=["cum", "nl"], writes=["colst"])
        K.ev("v", (lambda v, o=colst[:, :], i_=colst[:, :]:
                   v.tensor_scalar_add(o, i_, LNSCALE)),
             reads=["colst"], writes=["colst"])
        K.ev("s", (lambda s, o=expc[:, :], i_=cum[:, :]:
                   s.activation(o, i_, AF.Exp, scale=np.float32(-1.0))),
             reads=["cum"], writes=["expc"])

        for g in range(3):
            n_in = 3 if g < 2 else 2
            K.ev("v", (lambda v,
                       o=bass.AP(W["cumsc"], 0, [[128, 128], [32, n_in]]),
                       i_=cum[:, 3 * g:3 * g + n_in]:
                       v.tensor_scalar_mul(o, i_, -1.0)),
                 reads=["cum"], writes=["cumsc"])
            K.P.emit("t", (lambda t, o=K.pb[3][:, 128 + 128 * g:256 + 128 * g],
                           i_=W["cumsc"][:, :], idm=K.C["id_f32"][:, :]:
                           t.transpose(o, i_, idm)),
                     reads=["cumsc", "id_f32"], writes=[("pb", 3)])
            K.ev("s", (lambda s, o=cumT[:, g, :],
                       i_=K.pb[3][:, 128 + 128 * g:256 + 128 * g]: s.copy(o, i_)),
                 reads=[("pb", 3)], writes=[("cumT", g)])
        cumT_keys = [("cumT", g) for g in range(3)]

        for h in range(8):
            g, m = h // 3, h % 3
            bank = base_cls_bank[m]
            K.mm_group_ap(K.pb[bank][:, 384 + h % 4:385 + h % 4], ("pb", bank),
                          [(ONESF[32 * m:32 * m + 1, :],
                            cumT[32 * m:32 * m + 1, g, 127:128])],
                          ["ones_f32"] + cumT_keys)
            K.ev("v", (lambda v, o=ncC[:, h:h + 1],
                       i_=K.pb[bank][:, 384 + h % 4:385 + h % 4]:
                       v.tensor_copy(o, i_)),
                 reads=[("pb", bank)], writes=[("ncC", h)])
        ncC_keys = [("ncC", h) for h in range(8)]
        K.ev("v", (lambda v, o=wcol[:, :], a=colst[:, :], b=ncC[:, :]:
                   v.tensor_add(o, a, b)),
             reads=["colst"] + ncC_keys, writes=["wcol"])
        K.ev("s", (lambda s, o=wcol[:, :], i_=wcol[:, :]:
                   s.activation(o, i_, AF.Exp)),
             reads=["wcol"], writes=["wcol"])
        K.ev("s", (lambda s, o=decC[:, :], i_=ncC[:, :]:
                   s.activation(o, i_, AF.Exp)),
             reads=ncC_keys, writes=["decC"])

        for h in range(8):
            g, m = h // 3, h % 3
            bank = base_cls_bank[m]
            c0 = 128 * (h % 3 if m != 0 else (h // 3))
            slot = {0: h // 3, 1: h // 3, 2: h // 3}[m]
            c0 = 128 * slot
            K.mm_group_ap(K.pb[bank][:, c0:c0 + 128], ("pb", bank),
                          [(ONESF[32 * m:32 * m + 1, :],
                            cumT[32 * m:32 * m + 1, g, :])],
                          ["ones_f32"] + cumT_keys)
            K.ev("v", (lambda v, o=dsb[:, h, :], i0=K.pb[bank][:, c0:c0 + 128],
                       i1=MASKD[:, :]: v.tensor_tensor(o, i0, i1, ALU.add)),
                 reads=[("pb", bank), "maskd"], writes=[("dsb", h)])
            K.ev("s", (lambda s, o=dsb[:, h, :], i_=dsb[:, h, :],
                       b=colst[:, h:h + 1]:
                       s.activation(o, i_, AF.Exp, bias=b)),
                 reads=[("dsb", h), "colst"], writes=[("dsb", h)])
        dsb_keys = [("dsb", h) for h in range(8)]

        def proj_tm(wt_name, out_tile, out_key, rope):
            for bank, eg in ((0, 0), (1, 1)):
                terms = [(rfm[:, kc, :], W[wt_name][:, kc, 512 * eg:512 * eg + 512])
                         for kc in range(8)]
                K.mm_group(bank, 0, 512, terms,
                           [(wt_name, kc) for kc in range(8)] + rfm_keys)
            if not rope:
                for bank in range(2):
                    K.ev("v", (lambda v, o=out_tile[:, 512 * bank:512 * bank + 512],
                               i_=K.pb[bank][:, 0:512]: v.tensor_copy(o, i_)),
                         reads=[("pb", bank)], writes=[out_key])
                return
            cos = K.C["ropecos"]
            sin = K.C["ropesin"]
            for bank in range(2):
                ps = K.pb[bank]
                qq1 = bass.AP(ps, 0, [[512, 128], [128, 4], [1, 64]])
                qq2 = bass.AP(ps, 64, [[512, 128], [128, 4], [1, 64]])
                cosb = bass.AP(cos, 64 * ci, [[2048, 128], [0, 4], [1, 64]])
                sinb = bass.AP(sin, 64 * ci, [[2048, 128], [0, 4], [1, 64]])
                o1 = bass.AP(out_tile, 512 * bank, [[1024, 128], [128, 4], [1, 64]])
                o2 = bass.AP(out_tile, 512 * bank + 64, [[1024, 128], [128, 4], [1, 64]])
                t1 = te1[:, 0:256].rearrange("p (c w) -> p c w", w=64)
                t2 = te1[:, 256:512].rearrange("p (c w) -> p c w", w=64)
                t3 = te2[:, 0:256].rearrange("p (c w) -> p c w", w=64)
                t4 = te2[:, 256:512].rearrange("p (c w) -> p c w", w=64)
                K.ev("v", (lambda v, o=t1, i0=qq1, i1=cosb:
                           v.tensor_tensor(o, i0, i1, ALU.mult)),
                     reads=[("pb", bank), "ropecos"], writes=[("p2te1", 0)])
                K.ev("v", (lambda v, o=t2, i0=qq2, i1=sinb:
                           v.tensor_tensor(o, i0, i1, ALU.mult)),
                     reads=[("pb", bank), "ropesin"], writes=[("p2te1", 1)])
                K.ev("v", (lambda v, o=o1, a=t1, b=t2: v.tensor_sub(o, a, b)),
                     reads=[("p2te1", 0), ("p2te1", 1)], writes=[out_key])
                K.ev("v", (lambda v, o=t3, i0=qq1, i1=sinb:
                           v.tensor_tensor(o, i0, i1, ALU.mult)),
                     reads=[("pb", bank), "ropesin"], writes=[("p2te2", 0)])
                K.ev("v", (lambda v, o=t4, i0=qq2, i1=cosb:
                           v.tensor_tensor(o, i0, i1, ALU.mult)),
                     reads=[("pb", bank), "ropecos"], writes=[("p2te2", 1)])
                K.ev("v", (lambda v, o=o2, a=t3, b=t4: v.tensor_add(o, a, b)),
                     reads=[("p2te2", 0), ("p2te2", 1)], writes=[out_key])

        proj_tm("wq1T", q_ro, "q_ro", True)
        for half in range(2):
            bb = half
            for q in range(4):
                bc = 4 * half + q
                K.transpose(bb, 128 * q, q_ro[:, 128 * bc:128 * bc + 128], ["q_ro"])
            K.ev("s", (lambda s, o=qfm2[:, 4 * half:4 * half + 4, :],
                       i_=K.pbf[bb][:, 0:512]: s.copy(o, i_)),
                 reads=[("pbf", bb)], writes=["qfm2"])
        proj_tm("wk1T", k_ro, "k_ro", True)
        for half in range(2):
            bb = half
            for q in range(4):
                bc = 4 * half + q
                K.transpose(bb, 128 * q, k_ro[:, 128 * bc:128 * bc + 128], ["k_ro"])
            K.ev("s", (lambda s, o=kfm2[:, 4 * half:4 * half + 4, :],
                       i_=K.pbf[bb][:, 0:512]: s.copy(o, i_)),
                 reads=[("pbf", bb)], writes=["kfm2"])
        proj_tm("wv1T", v_bf, "v_bf", False)

        K.ev("s", (lambda s, o=W["M_bf"][:, :, :], i_=W["M_sb"][:, :, :]:
                   s.copy(o, i_)),
             reads=["M_sb"], writes=["M_bf"])

        for h in range(8):
            bank = h // 4
            c0 = 128 * (h % 4)
            K.mm_group(bank, c0, c0 + 128,
                       [(kfm2[:, h, :], qfm2[:, h, :])], ["kfm2", "qfm2"])
        for bank in range(2):
            K.ev("v", (lambda v, o=abuf2[:, 4 * bank:4 * bank + 4, :],
                       i0=K.pb[bank][:, 0:512].rearrange("p (h w) -> p h w", h=4),
                       i1=dsb[:, 4 * bank:4 * bank + 4, :]:
                       v.tensor_tensor(o, i0, i1, ALU.mult)),
                 reads=[("pb", bank)] + dsb_keys, writes=[("abuf2", bank)])

        for h in range(8):
            bank = 2 + h // 4
            c0 = 128 * (h % 4)
            K.mm_group(bank, c0, c0 + 128,
                       [(abuf2[:, h, :], v_bf[:, 128 * h:128 * h + 128])],
                       [("abuf2", h // 4), "v_bf"])
            bank2 = 4 + h // 4
            K.mm_group(bank2, c0, c0 + 128,
                       [(qfm2[:, h, :], W["M_bf"][:, h, :])], ["qfm2", "M_bf"])
        for h in range(8):
            bank2 = 4 + h // 4
            c0 = 128 * (h % 4)
            K.ev("s", (lambda s, o=itmp[:, 128 * h:128 * h + 128],
                       i_=K.pb[bank2][:, c0:c0 + 128], sca=expc[:, h:h + 1]:
                       s.activation(o, i_, AF.Copy, scale=sca)),
                 reads=[("pb", bank2), "expc"], writes=[("itmp", h)])
        itmp_keys = [("itmp", h) for h in range(8)]
        for bank in (2, 3):
            b = bank - 2
            K.ev("v", (lambda v, o=y0[:, 512 * b:512 * b + 512],
                       i0=K.pb[bank][:, 0:512], i1=itmp[:, 512 * b:512 * b + 512]:
                       v.tensor_tensor(o, i0, i1, ALU.add)),
                 reads=[("pb", bank)] + itmp_keys, writes=[("y0", b)])
            ogb = bass.AP(og, 4 * b, [[8, 128], [1, 4], [0, 128]])
            K.ev("v", (lambda v,
                       o=y_bf[:, 512 * b:512 * b + 512].rearrange("p (c w) -> p c w", w=128),
                       i0=y0[:, 512 * b:512 * b + 512].rearrange("p (c w) -> p c w", w=128),
                       i1=ogb: v.tensor_tensor(o, i0, i1, ALU.mult)),
                 reads=[("y0", b), "og"], writes=[("y_bf", b)])
        ybf_keys = [("y_bf", 0), ("y_bf", 1)]

        wrep = bass.AP(wcol, 0, [[8, 128], [1, 8], [0, 128]])
        K.ev("v", (lambda v, o=khat[:, :].rearrange("p (c w) -> p c w", w=128),
                   i0=k_ro[:, :].rearrange("p (c w) -> p c w", w=128), i1=wrep:
                   v.tensor_tensor(o, i0, i1, ALU.mult)),
             reads=["k_ro", "wcol"], writes=["khat"])
        for h in range(8):
            bank = h // 4
            c0 = 128 * (h % 4)
            K.mm_group(bank, c0, c0 + 128,
                       [(khat[:, 128 * h:128 * h + 128],
                         v_bf[:, 128 * h:128 * h + 128])], ["khat", "v_bf"])
        for h in range(8):
            bank = h // 4
            c0 = 128 * (h % 4)
            K.ev("v", (lambda v, o=W["M_sb"][:, h, :], i0=W["M_sb"][:, h, :],
                       sca=decC[:, h:h + 1], i1=K.pb[bank][:, c0:c0 + 128]:
                       v.scalar_tensor_tensor(o, i0, sca, i1, ALU.mult, ALU.add)),
                 reads=[("pb", bank), "M_sb", "decC"], writes=["M_sb"])

        for half in range(2):
            bb = half
            for q in range(4):
                bc = 4 * half + q
                K.transpose(bb, 128 * q, y_bf[:, 128 * bc:128 * bc + 128], ybf_keys)
            K.ev("s", (lambda s, o=yfm2[:, 4 * half:4 * half + 4, :],
                       i_=K.pbf[bb][:, 0:512]: s.copy(o, i_)),
                 reads=[("pbf", bb)], writes=["yfm2"])
        for bank, eg in ((2, 0), (3, 1)):
            terms = [(yfm2[:, kc, :], W["wout1T"][:, kc, 512 * eg:512 * eg + 512])
                     for kc in range(8)]
            terms.append((K.C["ones_bf"][0:1, 0:128],
                          W["bout"][0:1, 512 * eg:512 * eg + 512]))
            K.mm_group(bank, 0, 512, terms,
                       [("wout1T", kc) for kc in range(8)] + ["yfm2", "bout",
                                                             "ones_bf"])
            b = bank - 2
            K.ev("v", (lambda v, o=s1t[:, 512 * b:512 * b + 512],
                       i_=K.pb[bank][:, 0:512]: v.tensor_copy(o, i_)),
                 reads=[("pb", bank)], writes=[("s1t", b)])
        K.dma("p", K.io["s1"][rows, :], s1t[:, :],
              reads=[("s1t", 0), ("s1t", 1)], writes=[("s1_h", ci)])

        L = lg
        K.ev("v", (lambda v, o=L[:, 8:9], i_=L[:, 0:8]:
                   v.reduce_max(o, i_, axis=AX.X, negate=True)),
             reads=[("lg", "logits")], writes=[("lg", "nm")])
        K.ev("s", (lambda s, o=L[:, 16:24], i_=L[:, 0:8], b=L[:, 8:9]:
                   s.activation(o, i_, AF.Exp, bias=b)),
             reads=[("lg", "logits"), ("lg", "nm")], writes=[("lg", "ex")])
        K.ev("v", (lambda v, o=L[:, 9:10], i_=L[:, 16:24]:
                   v.reduce_sum(o, i_, axis=AX.X)),
             reads=[("lg", "ex")], writes=[("lg", "sum")])
        K.ev("v", (lambda v, o=L[:, 10:11], i_=L[:, 9:10]: v.reciprocal(o, i_)),
             reads=[("lg", "sum")], writes=[("lg", "rs")])
        K.ev("v", (lambda v, o=L[:, 16:24], i_=L[:, 16:24], s1=L[:, 10:11]:
                   v.tensor_scalar_mul(o, i_, s1)),
             reads=[("lg", "ex"), ("lg", "rs")], writes=[("lg", "sco")])
        K.ev("v", (lambda v, o=L[:, 24:32], a=L[:, 16:24], b=W["gb_bc"][:, :]:
                   v.tensor_add(o, a, b)),
             reads=[("lg", "sco"), "gb_bc"], writes=[("lg", "bi")])
        K.ev("v", (lambda v, o=L[:, 11:12], i_=L[:, 24:32]:
                   v.reduce_max(o, i_, axis=AX.X)),
             reads=[("lg", "bi")], writes=[("lg", "m1")])
        K.ev("v", (lambda v, o=L[:, 32:40], i_=L[:, 24:32], s1=L[:, 11:12]:
                   v.tensor_scalar(o, i_, s1, None, ALU.is_ge)),
             reads=[("lg", "bi"), ("lg", "m1")], writes=[("lg", "eq")])
        K.ev("v", (lambda v, o=L[:, 40:48], i0=L[:, 32:40], i1=idxp[:, :]:
                   v.scalar_tensor_tensor(o, i0, -1.0e4, i1, ALU.mult, ALU.add)),
             reads=[("lg", "eq"), "idxp"], writes=[("lg", "sel")])
        K.ev("v", (lambda v, o=L[:, 12:13], i_=L[:, 40:48]:
                   v.tensor_reduce(o, i_, axis=AX.X, op=ALU.min)),
             reads=[("lg", "sel")], writes=[("lg", "i1")])
        K.ev("v", (lambda v, o=L[:, 32:40], i_=idxc[:, :], s1=L[:, 12:13]:
                   v.tensor_scalar(o, i_, s1, None, ALU.is_equal)),
             reads=["idxc", ("lg", "i1")], writes=[("lg", "mk1")])
        K.ev("v", (lambda v, o=L[:, 24:32], i0=L[:, 32:40], i1=L[:, 24:32]:
                   v.scalar_tensor_tensor(o, i0, -1.0e30, i1, ALU.mult, ALU.add)),
             reads=[("lg", "mk1"), ("lg", "bi")], writes=[("lg", "bi2")])
        K.ev("v", (lambda v, o=L[:, 13:14], i_=L[:, 24:32]:
                   v.reduce_max(o, i_, axis=AX.X)),
             reads=[("lg", "bi2")], writes=[("lg", "m2")])
        K.ev("v", (lambda v, o=L[:, 40:48], i_=L[:, 24:32], s1=L[:, 13:14]:
                   v.tensor_scalar(o, i_, s1, None, ALU.is_ge)),
             reads=[("lg", "bi2"), ("lg", "m2")], writes=[("lg", "eq2")])
        K.ev("v", (lambda v, o=L[:, 40:48], i0=L[:, 40:48], i1=idxp[:, :]:
                   v.scalar_tensor_tensor(o, i0, -1.0e4, i1, ALU.mult, ALU.add)),
             reads=[("lg", "eq2"), "idxp"], writes=[("lg", "sel2")])
        K.ev("v", (lambda v, o=L[:, 14:15], i_=L[:, 40:48]:
                   v.tensor_reduce(o, i_, axis=AX.X, op=ALU.min)),
             reads=[("lg", "sel2")], writes=[("lg", "i2")])
        K.ev("v", (lambda v, o=L[:, 40:48], i_=idxc[:, :], s1=L[:, 14:15]:
                   v.tensor_scalar(o, i_, s1, None, ALU.is_equal)),
             reads=["idxc", ("lg", "i2")], writes=[("lg", "mk2")])
        K.ev("v", (lambda v, o=L[:, 32:40], a=L[:, 32:40], b=L[:, 40:48]:
                   v.tensor_add(o, a, b)),
             reads=[("lg", "mk1"), ("lg", "mk2")], writes=[("lg", "mk")])
        K.ev("v", (lambda v, o=cwt[:, :], a=L[:, 32:40], b=L[:, 16:24]:
                   v.tensor_mul(o, a, b)),
             reads=[("lg", "mk"), ("lg", "sco")], writes=["cwt"])
        if DEBUG and ci == DBG_CI:
            K.dbg("lg", lg[:, :], [128, 48], F32,
                  [("lg", "logits"), ("lg", "sco"), ("lg", "mk")])
            K.dbg("cwt", cwt[:, :], [128, 8], F32, ["cwt"])
            K.dbg("zg", zg[:, :], [128, 24], F32, ["zg"])
            K.dbg("nl", nl[:, :], [128, 16], F32, ["nl"])
            K.dbg("cum", cum[:, :], [128, 8], F32, ["cum"])
            K.dbg("og2", og[:, :], [128, 8], F32, ["og"])
            K.dbg("rsc", sc[:, :], [128, 8], F32, [("sc", 2), ("sc", 3)])
            K.dbg("gbbc", W["gb_bc"][:, :], [128, 8], F32, ["gb_bc"])
            K.dbg("idxp", idxp[:, :], [128, 8], F32, ["idxp"])
            K.dbg("idxc", idxc[:, :], [128, 8], F32, ["idxc"])
        K.dma("p", K.io["cwb"][rows, :], cwt[:, :], reads=["cwt"],
              writes=[("cw_h", ci)])

        if moe_casts:
            while moe_casts and len(moe_casts) > (nch - 1 - ci) * 2:
                moe_casts.pop(0)()

    P.barrier()
    ar.release()


def decl_io_A(nc):
    io = {}
    io["x"] = nc.dram_tensor("x", [T, D], F32, kind="ExternalInput").ap()
    for nm, hbm in (("wq0", "L0_attn_Wq"), ("wk0", "L0_attn_Wk"),
                    ("wv0", "L0_attn_Wv"), ("wo0", "L0_attn_Wo")):
        io[nm] = nc.dram_tensor(hbm, [1024, 1024], F32, kind="ExternalInput").ap()
    io["wfr"] = nc.dram_tensor("L0_ffn_w_freq_real", [1024, 1024], F32,
                               kind="ExternalInput").ap()
    io["wfi"] = nc.dram_tensor("L0_ffn_w_freq_imag", [1024, 1024], F32,
                               kind="ExternalInput").ap()
    io["fbias"] = nc.dram_tensor("L0_ffn_bias", [1024], F32, kind="ExternalInput").ap()
    for nm in ("c1", "c2", "c2n", "ut01", "id_bf"):
        shape, dt = CONST_SPECS[nm]
        io[nm] = nc.dram_tensor(nm, shape, dt, kind="ExternalInput").ap()
    return io


def build_A(outputs_external=True, nch=NCH):
    nc = bass.Bass("TRN2", target_bir_lowering=False)
    K = Ker(nc)
    K.io = decl_io_A(nc)
    kind = "ExternalOutput" if outputs_external else "Internal"
    K.io["x1"] = nc.dram_tensor("x1", [T, D], F32, kind=kind).ap()
    K.io["xn1f"] = nc.dram_tensor("xn1f", [T, D], F32, kind=kind).ap()
    K.io["xn1b"] = nc.dram_tensor("xn1b", [T, D], BF16, kind=kind).ap()
    W = phase0_l0(K)
    phase1(K, W, moe_casts=None, nch=nch)
    K.P.build()
    return nc




def decl_io_B(nc):
    io = {}
    io["xn1f"] = nc.dram_tensor("xn1f", [T, D], F32, kind="ExternalInput").ap()
    for nm, hbm, shape in (
            ("wq1", "L1_attn_Wq", [1024, 1024]), ("wk1", "L1_attn_Wk", [1024, 1024]),
            ("wv1", "L1_attn_Wv", [1024, 1024]), ("wout1", "L1_attn_Wout", [1024, 1024]),
            ("wbeta", "L1_attn_Wbeta", [8, 1024]), ("wig", "L1_attn_Wig", [8, 1024]),
            ("wog", "L1_attn_Wog", [8, 1024]), ("gate_w", "L1_ffn_gate_w", [8, 1024]),
            ("norm_w", "L1_attn_norm_w", [1024]), ("bbeta", "L1_attn_bbeta", [8]),
            ("big", "L1_attn_big", [8]), ("bog", "L1_attn_bog", [8]),
            ("bout", "L1_attn_bout", [1024]), ("gate_b", "L1_ffn_gate_b", [8])):
        io[nm] = nc.dram_tensor(hbm, shape, F32, kind="ExternalInput").ap()
    for nm in ("cumm", "maskd", "id_f32", "id_bf", "ropecos", "ropesin"):
        shape, dt = CONST_SPECS[nm]
        io[nm] = nc.dram_tensor(nm, shape, dt, kind="ExternalInput").ap()
    return io


def build_B(nch=NCH):
    nc = bass.Bass("TRN2", target_bir_lowering=False)
    K = Ker(nc)
    K.io = decl_io_B(nc)
    K.io["s1"] = nc.dram_tensor("s1", [T, D], F32, kind="ExternalOutput").ap()
    K.io["cwb"] = nc.dram_tensor("cwb", [T, 8], F32, kind="ExternalOutput").ap()
    W = phase2_prep(K)
    phase2(K, W, moe_casts=None, nch=nch)
    K.P.build()
    return nc





def phase4(K, run_casts=False):
    nc, P, ar = K.nc, K.P, K.ar
    if run_casts or not hasattr(K, "moe_cache"):
        thunks = moe_cast_list(K)
        for th in thunks:
            th()
    w1c, w3c, w2c = K.moe_cache["w1c"], K.moe_cache["w3c"], K.moe_cache["w2c"]
    ar.mark()
    if "ones_bf" not in K.C:
        ones_bf = ar.alloc("ones_bf", [1, 512], BF16)
        K.C["ones_bf"] = ones_bf
        K.ev("p", lambda g: g.memset(ones_bf[:, :], 1.0), writes=("ones_bf",))
    if "ones_f32" not in K.C:
        ones_f32 = ar.alloc("ones_f32", [128, 128], F32)
        K.C["ones_f32"] = ones_f32
        K.ev("p", lambda g: g.memset(ones_f32[:, :], 1.0), writes=("ones_f32",))
    ONESF = K.C["ones_f32"]

    bcols = ar.alloc("bcols", [128, 9, 32], F32)
    ar.mark()
    bstg = ar.alloc("bstg", [1, 2048], F32)
    brow = ar.alloc("brow", [1, 2048], F32)
    for e in range(9):
        for t, src in ((0, "b1"), (1, "b3")):
            if e < 8:
                row_ap = K.io[src][e, :].rearrange("(a d) -> a d", a=1)
            else:
                row_ap = K.io["s" + src].rearrange("(a d) -> a d", a=1)
            K.dma("y", bstg[:, :], row_ap, writes=["bstg"])
            K.ev("v", (lambda v, o=brow[:, 0:1024],
                       i_=bass.AP(bstg, 0, [[2048, 1], [2, 1024]]):
                       v.tensor_copy(o, i_)),
                 reads=["bstg"], writes=[("brow", 0)])
            K.ev("v", (lambda v, o=brow[:, 1024:2048],
                       i_=bass.AP(bstg, 1, [[2048, 1], [2, 1024]]):
                       v.tensor_copy(o, i_)),
                 reads=["bstg"], writes=[("brow", 1)])
            for half in range(2):
                for fc in range(8):
                    col = (2 * t + half) * 8 + fc
                    K.mm_group_ap(K.pb[0][:, col:col + 1], ("pb", 0),
                                  [(brow[0:1, 1024 * half + 128 * fc:
                                         1024 * half + 128 * fc + 128],
                                    ONESF[0:1, 0:1])],
                                  [("brow", half), "ones_f32"])
            K.ev("v", (lambda v, o=bcols[:, e, 16 * t:16 * t + 8],
                       i_=K.pb[0][:, 16 * t:16 * t + 8]: v.tensor_copy(o, i_)),
                 reads=[("pb", 0)], writes=[("bcols", e, t, 0)])
            K.ev("v", (lambda v, o=bass.AP(bcols, 32 * e + 8 + 16 * t,
                                           [[288, 128], [1, 8]]),
                       i_=K.pb[0][:, 8 + 16 * t:16 + 16 * t]: v.tensor_copy(o, i_)),
                 reads=[("pb", 0)], writes=[("bcols", e, t, 1)])
    bcol_keys = [("bcols", e, t, hh) for e in range(9) for t in range(2)
                 for hh in range(2)]
    P.barrier()
    ar.release()
    b2r = [ar.alloc(f"b2r{i}", [1, 1024], BF16) for i in range(2)]

    xn1fm = ar.alloc("xn1fm", [128, 8, 1024], BF16)
    cwcol = ar.alloc("cwcol", [128, 64], F32)
    acc = ar.alloc("acc", [128, 8, 1024], F32)
    gsig = ar.alloc("gsig", [128, 8, 1024], BF16)
    abuf = ar.alloc("p4a", [128, 8, 1024], BF16)
    wb = [ar.alloc(f"wb{i}", [128, 8, 1024], BF16) for i in range(4)]
    wb2 = ar.alloc("wb2", [128, 8, 1024], BF16)
    tb1 = ar.alloc("tb1", [128, 1024], F32)
    tb2 = ar.alloc("tb2", [128, 1024], F32)
    tb3 = ar.alloc("tb3", [128, 1024], F32)
    xf = [ar.alloc(f"p4x{i}", [128, 1024], F32) for i in range(2)]
    sf = [ar.alloc(f"p4s{i}", [128, 1024], F32) for i in range(2)]

    def load_h(slot, e, half, src, q):
        for dc in range(8):
            K.dma_t(q, wb[slot][:, dc, :],
                    src[e, 1024 * half:1024 * half + 1024,
                        128 * dc:128 * dc + 128],
                    reads=[(("w1c" if src is w1c else "w3c"), e, half)],
                    writes=[("wb", slot, dc)])
        return [("wb", slot, dc) for dc in range(8)]

    def load_w2(e, q):
        for dc in range(8):
            K.dma_t(q, wb2[:, dc, :], w2c[e, 0:1024, 128 * dc:128 * dc + 128],
                    reads=[("w2c", e)], writes=[("wb2", dc)])
        return [("wb2", dc) for dc in range(8)]

    NEXP = 9
    for p in range(NPASS):
        rows = slice(PASS_TOK * p, PASS_TOK * (p + 1))
        for dc in range(8):
            K.dma_t("y", xn1fm[:, dc, :],
                    K.io["xn1b"][rows, 128 * dc:128 * dc + 128],
                    reads=["xn1b_all"], writes=[("xn1fm", dc)])
        xn_keys = [("xn1fm", dc) for dc in range(8)]
        K.dma("y", cwcol[:, :].rearrange("p (tt e) -> p tt e", e=8),
              K.io["cwb"][rows, :].rearrange("(tt q) e -> q tt e", q=128),
              reads=["cwb_all"], writes=["cwcol"])
        K.ev("v", lambda v, o=acc[:, :, :]: v.memset(o, 0.0), writes=["acc"])

        for e in range(NEXP):
            sl_e = (0, 1)
            sl_o = (2, 3)
            if e == 0 and p == 0:
                k1e = load_h(sl_e[0], 0, 0, w1c, "y")
                k3e = load_h(sl_e[1], 0, 0, w3c, "y")
            else:
                k1e = [("wb", sl_e[0], dc) for dc in range(8)]
                k3e = [("wb", sl_e[1], dc) for dc in range(8)]
            k1o = load_h(sl_o[0], e, 1, w1c, "y")
            k3o = load_h(sl_o[1], e, 1, w3c, "y")
            kw2 = load_w2(e, "y")
            b2src = K.io["b2"][e, :] if e < 8 else K.io["sb2"]
            K.dma("p", b2r[e % 2][:, :], b2src.rearrange("(a d) -> a d", a=1),
                  writes=[("b2r", e % 2)])

            for fe in range(8):
                for tg in range(2):
                    b_h1 = tg
                    b_h3 = 2 + tg
                    terms1 = [(wb[sl_e[0]][:, dc, 128 * fe:128 * fe + 128],
                               xn1fm[:, dc, 512 * tg:512 * tg + 512])
                              for dc in range(8)]
                    K.mm_group(b_h1, 0, 512, terms1, k1e + xn_keys)
                    terms3 = [(wb[sl_e[1]][:, dc, 128 * fe:128 * fe + 128],
                               xn1fm[:, dc, 512 * tg:512 * tg + 512])
                              for dc in range(8)]
                    K.mm_group(b_h3, 0, 512, terms3, k3e + xn_keys)
                    bc1 = bcols[:, e, 0 * 8 + fe:0 * 8 + fe + 1]
                    bc3 = bcols[:, e, 16 + fe:16 + fe + 1]
                    t1s = tb1[:, 512 * tg:512 * tg + 512]
                    t2s = tb2[:, 512 * tg:512 * tg + 512]
                    t3s = tb3[:, 512 * tg:512 * tg + 512]
                    K.ev("s", (lambda s, o=t1s, i_=K.pb[b_h1][:, 0:512],
                               b=bc1: s.activation(o, i_, AF.Identity, bias=b)),
                         reads=[("pb", b_h1)] + bcol_keys, writes=[("tb1", tg)])
                    K.ev("v", (lambda v, o=t2s, i0=K.pb[b_h3][:, 0:512],
                               sca=bc3, i1=t1s:
                               v.scalar_tensor_tensor(o, i0, sca, i1, ALU.add,
                                                      ALU.mult)),
                         reads=[("pb", b_h3), ("tb1", tg)] + bcol_keys,
                         writes=[("tb2", tg)])
                    K.ev("v", (lambda v, o=t2s, i_=t2s:
                               v.tensor_scalar_min(o, i_, 7.0)),
                         reads=[("tb2", tg)], writes=[("tb2", tg)])
                    K.ev("s", (lambda s, o=t3s, i_=t2s:
                               s.activation(o, i_, AF.Sigmoid,
                                            scale=np.float32(1.702))),
                         reads=[("tb2", tg)], writes=[("tb3", tg)])
                    K.ev("v", (lambda v, o=gsig[:, fe, 512 * tg:512 * tg + 512],
                               a=t2s, b=t3s:
                               v.tensor_mul(o, a, b)),
                         reads=[("tb2", tg), ("tb3", tg)], writes=[("gsig", fe, tg)])
            gs_keys = [("gsig", fe, tg) for fe in range(8) for tg in range(2)]

            for fe in range(8):
                for tg in range(2):
                    b_h1 = tg
                    b_h3 = 2 + tg
                    terms1 = [(wb[sl_o[0]][:, dc, 128 * fe:128 * fe + 128],
                               xn1fm[:, dc, 512 * tg:512 * tg + 512])
                              for dc in range(8)]
                    K.mm_group(b_h1, 0, 512, terms1, k1o + xn_keys)
                    terms3 = [(wb[sl_o[1]][:, dc, 128 * fe:128 * fe + 128],
                               xn1fm[:, dc, 512 * tg:512 * tg + 512])
                              for dc in range(8)]
                    K.mm_group(b_h3, 0, 512, terms3, k3o + xn_keys)
                    bc1 = bcols[:, e, 8 + fe:8 + fe + 1]
                    bc3 = bcols[:, e, 24 + fe:24 + fe + 1]
                    t1s = tb1[:, 512 * tg:512 * tg + 512]
                    t2s = tb2[:, 512 * tg:512 * tg + 512]
                    K.ev("s", (lambda s, o=t1s, i_=K.pb[b_h1][:, 0:512],
                               b=bc1: s.activation(o, i_, AF.Identity, bias=b)),
                         reads=[("pb", b_h1)] + bcol_keys, writes=[("tb1", tg)])
                    K.ev("v", (lambda v, o=t2s, i0=K.pb[b_h3][:, 0:512],
                               sca=bc3, i1=t1s:
                               v.scalar_tensor_tensor(o, i0, sca, i1, ALU.add,
                                                      ALU.mult)),
                         reads=[("pb", b_h3), ("tb1", tg)] + bcol_keys,
                         writes=[("tb2", tg)])
                    K.ev("v", (lambda v, o=t2s, i_=t2s:
                               v.tensor_scalar(o, i_, 7.0, -7.0, ALU.min, ALU.max)),
                         reads=[("tb2", tg)], writes=[("tb2", tg)])
                    K.ev("v", (lambda v, o=abuf[:, fe, 512 * tg:512 * tg + 512],
                               i0=t2s,
                               i1=gsig[:, fe, 512 * tg:512 * tg + 512]:
                               v.scalar_tensor_tensor(o, i0, 1.0, i1, ALU.add,
                                                      ALU.mult)),
                         reads=[("tb2", tg), ("gsig", fe, tg)],
                         writes=[("p4a", fe, tg)])
            a_keys = [("p4a", fe, tg) for fe in range(8) for tg in range(2)]
            if DEBUG and p == 0 and e == 0:
                K.dbg("gsig0", gsig[:, :, :], [128, 8, 1024], BF16, gs_keys)
                K.dbg("a0", abuf[:, :, :], [128, 8, 1024], BF16, a_keys)
                K.dbg("bcols", bcols[:, :, :], [128, 9, 32], F32, bcol_keys)
                K.dbg("xnfm0", xn1fm[:, :, :], [128, 8, 1024], BF16, xn_keys)
                K.dbg("cwcol0", cwcol[:, :], [128, 64], F32, ["cwcol"])
            if e + 1 < NEXP:
                load_h(0, e + 1, 0, w1c, "y")
                load_h(1, e + 1, 0, w3c, "y")
            elif p + 1 < NPASS:
                load_h(0, 0, 0, w1c, "y")
                load_h(1, 0, 0, w3c, "y")

            for tt in range(8):
                for dh in range(2):
                    bank = 4 + dh
                    terms = [(abuf[:, fp, 128 * tt:128 * tt + 128],
                              wb2[:, fp, 512 * dh:512 * dh + 512])
                             for fp in range(8)]
                    terms.append((K.C["ones_bf"][0:1, 0:128],
                                  b2r[e % 2][0:1, 512 * dh:512 * dh + 512]))
                    K.mm_group(bank, 0, 512, terms,
                               a_keys + kw2 + ["ones_bf", ("b2r", e % 2)])
                    if e < 8:
                        cwc = cwcol[:, 8 * tt + e:8 * tt + e + 1]
                        K.ev("s", (lambda s, o=tb1[:, 512 * dh:512 * dh + 512],
                                   i_=K.pb[bank][:, 0:512], sca=cwc:
                                   s.activation(o, i_, AF.Copy, scale=sca)),
                             reads=[("pb", bank), "cwcol"], writes=[("tb1", dh)])
                    else:
                        K.ev("s", (lambda s, o=tb1[:, 512 * dh:512 * dh + 512],
                                   i_=K.pb[bank][:, 0:512]: s.copy(o, i_)),
                             reads=[("pb", bank)], writes=[("tb1", dh)])
                    K.ev("v", (lambda v, o=acc[:, tt, 512 * dh:512 * dh + 512],
                               a=acc[:, tt, 512 * dh:512 * dh + 512],
                               b=tb1[:, 512 * dh:512 * dh + 512]:
                               v.tensor_add(o, a, b)),
                         reads=[("tb1", dh), "acc"], writes=["acc"])

        if DEBUG and p == 0:
            K.dbg("accp0", acc[:, :, :], [128, 8, 1024], F32, ["acc"])
        for tt in range(8):
            par = tt % 2
            trows = slice(PASS_TOK * p + 128 * tt, PASS_TOK * p + 128 * tt + 128)
            K.dma("y", xf[par][:, :], K.io["x1"][trows, :],
                  reads=["x1_all"], writes=[(f"p4x{par}",)])
            K.dma("s", sf[par][:, :], K.io["s1"][trows, :],
                  reads=["s1_all"], writes=[(f"p4s{par}",)])
            K.ev("s", (lambda s, o=tb1[:, :], i_=acc[:, tt, :]:
                       s.activation(o, i_, AF.Sigmoid)),
                 reads=["acc"], writes=[("tb1", 0), ("tb1", 1)])
            K.ev("v", (lambda v, o=tb2[:, :], a=acc[:, tt, :], b=tb1[:, :]:
                       v.tensor_mul(o, a, b)),
                 reads=["acc", ("tb1", 0), ("tb1", 1)], writes=[("tb2", 0), ("tb2", 1)])
            K.ev("v", (lambda v, o=tb1[:, :], a=sf[par][:, :], b=tb2[:, :]:
                       v.tensor_mul(o, a, b)),
                 reads=[(f"p4s{par}",), ("tb2", 0), ("tb2", 1)],
                 writes=[("tb1", 0), ("tb1", 1)])
            K.ev("v", (lambda v, o=xf[par][:, :], a=xf[par][:, :], b=tb1[:, :]:
                       v.tensor_add(o, a, b)),
                 reads=[(f"p4x{par}",), ("tb1", 0), ("tb1", 1)],
                 writes=[(f"p4x{par}",)])
            K.dma("p", K.io["out"][trows, :], xf[par][:, :],
                  reads=[(f"p4x{par}",)], writes=[("out_h", p, tt)])

    P.barrier()
    ar.release()


def decl_io_C(nc):
    io = {}
    io["x1"] = nc.dram_tensor("x1", [T, D], F32, kind="ExternalInput").ap()
    io["s1"] = nc.dram_tensor("s1", [T, D], F32, kind="ExternalInput").ap()
    io["xn1b"] = nc.dram_tensor("xn1b", [T, D], BF16, kind="ExternalInput").ap()
    io["cwb"] = nc.dram_tensor("cwb", [T, 8], F32, kind="ExternalInput").ap()
    for nm, hbm, shape in (
            ("w1", "L1_ffn_w1", [8, 2048, 1024]), ("w3", "L1_ffn_w3", [8, 2048, 1024]),
            ("w2", "L1_ffn_w2", [8, 1024, 1024]), ("b1", "L1_ffn_b1", [8, 2048]),
            ("b3", "L1_ffn_b3", [8, 2048]), ("b2", "L1_ffn_b2", [8, 1024]),
            ("sw1", "L1_ffn_sw1", [2048, 1024]), ("sw3", "L1_ffn_sw3", [2048, 1024]),
            ("sw2", "L1_ffn_sw2", [1024, 1024]), ("sb1", "L1_ffn_sb1", [2048]),
            ("sb3", "L1_ffn_sb3", [2048]), ("sb2", "L1_ffn_sb2", [1024])):
        io[nm] = nc.dram_tensor(hbm, shape, F32, kind="ExternalInput").ap()
    return io


def build_C():
    nc = bass.Bass("TRN2", target_bir_lowering=False)
    K = Ker(nc)
    K.io = decl_io_C(nc)
    K.io["out"] = nc.dram_tensor("out", [T, D], F32, kind="ExternalOutput").ap()
    phase4(K, run_casts=True)
    K.P.build()
    return nc


def prelude(K):
    for nm in ("id_bf",):
        load_const(K, nm)
    ones_bf = K.ar.alloc("ones_bf", [1, 512], BF16)
    K.C["ones_bf"] = ones_bf
    K.ev("p", lambda g: g.memset(ones_bf[:, :], 1.0), writes=("ones_bf",))
    ones_f32 = K.ar.alloc("ones_f32", [128, 128], F32)
    K.C["ones_f32"] = ones_f32
    K.ev("p", lambda g: g.memset(ones_f32[:, :], 1.0), writes=("ones_f32",))


def build_full():
    nc = bass.Bass("TRN2", target_bir_lowering=False)
    K = Ker(nc)
    io = {}
    io.update(decl_io_A(nc))
    for nm, hbm, shape in (
            ("wq1", "L1_attn_Wq", [1024, 1024]), ("wk1", "L1_attn_Wk", [1024, 1024]),
            ("wv1", "L1_attn_Wv", [1024, 1024]), ("wout1", "L1_attn_Wout", [1024, 1024]),
            ("wbeta", "L1_attn_Wbeta", [8, 1024]), ("wig", "L1_attn_Wig", [8, 1024]),
            ("wog", "L1_attn_Wog", [8, 1024]), ("gate_w", "L1_ffn_gate_w", [8, 1024]),
            ("norm_w", "L1_attn_norm_w", [1024]), ("bbeta", "L1_attn_bbeta", [8]),
            ("big", "L1_attn_big", [8]), ("bog", "L1_attn_bog", [8]),
            ("bout", "L1_attn_bout", [1024]), ("gate_b", "L1_ffn_gate_b", [8])):
        io[nm] = nc.dram_tensor(hbm, shape, F32, kind="ExternalInput").ap()
    for nm in ("ropecos", "ropesin", "cumm", "maskd", "id_f32"):
        shape, dt = CONST_SPECS[nm]
        io[nm] = nc.dram_tensor(nm, shape, dt, kind="ExternalInput").ap()
    for nm, hbm, shape in (
            ("w1", "L1_ffn_w1", [8, 2048, 1024]), ("w3", "L1_ffn_w3", [8, 2048, 1024]),
            ("w2", "L1_ffn_w2", [8, 1024, 1024]), ("b1", "L1_ffn_b1", [8, 2048]),
            ("b3", "L1_ffn_b3", [8, 2048]), ("b2", "L1_ffn_b2", [8, 1024]),
            ("sw1", "L1_ffn_sw1", [2048, 1024]), ("sw3", "L1_ffn_sw3", [2048, 1024]),
            ("sw2", "L1_ffn_sw2", [1024, 1024]), ("sb1", "L1_ffn_sb1", [2048]),
            ("sb3", "L1_ffn_sb3", [2048]), ("sb2", "L1_ffn_sb2", [1024])):
        io[nm] = nc.dram_tensor(hbm, shape, F32, kind="ExternalInput").ap()
    io["x1"] = nc.dram_tensor("x1", [T, D], F32, kind="Internal").ap()
    io["xn1f"] = nc.dram_tensor("xn1f", [T, D], F32, kind="Internal").ap()
    io["xn1b"] = nc.dram_tensor("xn1b", [T, D], BF16, kind="Internal").ap()
    io["s1"] = nc.dram_tensor("s1", [T, D], F32, kind="Internal").ap()
    io["cwb"] = nc.dram_tensor("cwb", [T, 8], F32, kind="Internal").ap()
    io["out"] = nc.dram_tensor("out", [T, D], F32, kind="ExternalOutput").ap()
    K.io = io

    prelude(K)
    casts = moe_cast_list(K)
    K.ar.mark()
    W0 = phase0_l0(K)
    phase1(K, W0, moe_casts=casts, cast_floor=22)
    K.ar.release()
    K.ar.mark()
    W1 = phase2_prep(K)
    phase2(K, W1, moe_casts=casts)
    K.ar.release()
    phase4(K)
    K.P.build()
    return nc


def kernel_fused(**inputs):
    consts = _consts()
    x_full = inputs["L0_attn_x"]
    cores = list(range(NCORES))
    nc = build_full()
    ims = []
    for c in cores:
        m = {"x": np.ascontiguousarray(x_full[c])}
        for nm in inputs:
            if nm != "L0_attn_x":
                m[nm] = inputs[nm]
        for nm in ("c1", "c2", "c2n", "ut01", "id_bf", "id_f32", "cumm",
                   "maskd", "ropecos", "ropesin"):
            m[nm] = consts[nm]
        ims.append(m)
    res = run_bass_kernel_spmd(nc, ims, core_ids=cores)
    return np.stack([res.results[c]["out"] for c in cores], axis=0)


FUSED = True

_CONSTS = None


def _consts():
    global _CONSTS
    if _CONSTS is None:
        _CONSTS = host_constants()
    return _CONSTS


def kernel(**inputs):
    if FUSED:
        return kernel_fused(**inputs)
    consts = _consts()
    x_full = inputs["L0_attn_x"]
    cores = list(range(NCORES))

    ncA = build_A()
    imA = []
    for c in cores:
        m = {"x": np.ascontiguousarray(x_full[c])}
        for nm in ("L0_attn_Wq", "L0_attn_Wk", "L0_attn_Wv", "L0_attn_Wo",
                   "L0_ffn_w_freq_real", "L0_ffn_w_freq_imag", "L0_ffn_bias"):
            m[nm] = inputs[nm]
        for nm in ("c1", "c2", "c2n", "ut01", "id_bf"):
            m[nm] = consts[nm]
        imA.append(m)
    resA = run_bass_kernel_spmd(ncA, imA, core_ids=cores)

    ncB = build_B()
    imB = []
    for c in cores:
        m = {"xn1f": resA.results[c]["xn1f"]}
        for nm in ("L1_attn_Wq", "L1_attn_Wk", "L1_attn_Wv", "L1_attn_Wout",
                   "L1_attn_Wbeta", "L1_attn_Wig", "L1_attn_Wog",
                   "L1_ffn_gate_w", "L1_attn_norm_w", "L1_attn_bbeta",
                   "L1_attn_big", "L1_attn_bog", "L1_attn_bout", "L1_ffn_gate_b"):
            m[nm] = inputs[nm]
        for nm in ("cumm", "maskd", "id_f32", "id_bf", "ropecos", "ropesin"):
            m[nm] = consts[nm]
        imB.append(m)
    resB = run_bass_kernel_spmd(ncB, imB, core_ids=cores)

    ncC = build_C()
    imC = []
    for c in cores:
        m = {"x1": resA.results[c]["x1"], "s1": resB.results[c]["s1"],
             "xn1b": resA.results[c]["xn1b"], "cwb": resB.results[c]["cwb"]}
        for nm in ("L1_ffn_w1", "L1_ffn_w3", "L1_ffn_w2", "L1_ffn_b1",
                   "L1_ffn_b3", "L1_ffn_b2", "L1_ffn_sw1", "L1_ffn_sw3",
                   "L1_ffn_sw2", "L1_ffn_sb1", "L1_ffn_sb3", "L1_ffn_sb2"):
            m[nm] = inputs[nm]
        imC.append(m)
    resC = run_bass_kernel_spmd(ncC, imC, core_ids=cores)

    return np.stack([resC.results[c]["out"] for c in cores], axis=0)
